# version 05
# baseline (speedup 1.0000x reference)
import numpy as np
import ml_dtypes
from collections import defaultdict
from contextlib import ExitStack

import concourse.bass as bass
import concourse.mybir as mybir
from concourse.bass_utils import run_bass_kernel_spmd

F32 = mybir.dt.float32
BF16 = mybir.dt.bfloat16
AF = mybir.ActivationFunctionType
ALU = mybir.AluOpType
AX = mybir.AxisListType

S = 8192
T = 2048
D = 1024
NHEADS = 16
KC = 64
NTILE = 16
NE_R = 8
NE_S = 2
F_FFN = 4096
NEG = -3.0e38
EPS = 1e-5


class _Res:
    __slots__ = ("writer", "readers")

    def __init__(self):
        self.writer = None
        self.readers = []


class Plan:
    ENGINES = {"pe": 1, "act": 1, "dve": 1, "sp": 16, "gp": 16}

    def __init__(self, nc, ctx):
        self.nc = nc
        self.sems = {}
        self.counts = {}
        self.ops = {e: [] for e in self.ENGINES}
        self.satisfied = {e: defaultdict(int) for e in self.ENGINES}
        for e in self.ENGINES:
            self.sems[e] = ctx.enter_context(nc.semaphore(f"sem_{e}"))
            self.counts[e] = 0
        self._res = {}
        self._fence = None
        self._fence_applied = set()
        self._gseq = 0
        self._dma_seq = {"gp": {}, "sp": {}}

    def res(self, name):
        r = self._res.get(name)
        if r is None:
            r = _Res()
            self._res[name] = r
        return r

    def fence(self):
        self._fence = [(e, self.counts[e]) for e in self.ENGINES if self.counts[e]]
        self._fence_applied = set()

    def op(self, eng, emit, reads=(), writes=(), extra_waits=(), force_sync=False,
           tag=""):
        reads = [self.res(r) if isinstance(r, str) else r for r in reads]
        writes = [self.res(r) if isinstance(r, str) else r for r in writes]
        want = {}

        def need(tok):
            if tok is None:
                return
            s, c = tok
            if want.get(s, 0) < c:
                want[s] = c

        for r in reads:
            need(r.writer)
        for r in writes:
            need(r.writer)
            for t in r.readers:
                need(t)
        for t in extra_waits:
            need(t)
        if self._fence is not None and eng not in self._fence_applied:
            for t in self._fence:
                need(t)
            self._fence_applied.add(eng)
            if len(self._fence_applied) == len(self.ENGINES):
                self._fence = None

        waits = []
        sat = self.satisfied[eng]
        for s, c in sorted(want.items()):
            if s == eng and not force_sync and c <= self.counts[eng]:
                continue
            if s == eng and force_sync and c <= self.counts[eng]:
                if sat[s] >= c:
                    continue
                waits.append((s, c))
                sat[s] = c
                continue
            if sat[s] >= c:
                continue
            waits.append((s, c))
            sat[s] = c

        step = self.ENGINES[eng]
        self.counts[eng] += step
        tok = (eng, self.counts[eng])
        seq = self._gseq
        self._gseq += 1
        if step == 16:
            self._dma_seq[eng][self.counts[eng]] = seq
        self.ops[eng].append((waits, emit, step, tag, seq))
        for r in reads:
            r.readers.append(tok)
        for r in writes:
            r.writer = tok
            r.readers = []
        return tok

    def emit_engine(self, eng, obj):
        sems = self.sems
        for waits, emit, step, tag, seq in self.ops[eng]:
            for s, c in waits:
                if s in ("gp", "sp"):
                    nseq = self._dma_seq[s].get(c + 16)
                    if nseq is not None and nseq < seq:
                        c = c + 16
                obj.wait_ge(sems[s], c)
            last = emit(obj)
            assert last is not None, f"emit {tag!r} on {eng} returned None"
            last.then_inc(sems[eng], step)


def build_program():
    nc = bass.Bass("TRN2", target_bir_lowering=False, debug=False)

    xt = nc.dram_tensor("xt", [D, S], F32, kind="ExternalInput").ap()
    xto = nc.dram_tensor("xt_own", [D, T], F32, kind="ExternalInput").ap()
    xtm = nc.dram_tensor("xtm", [T, D], F32, kind="ExternalInput").ap()
    wqd = nc.dram_tensor("wq", [D, D], F32, kind="ExternalInput").ap()
    wkd = nc.dram_tensor("wk", [D, D], F32, kind="ExternalInput").ap()
    wvd = nc.dram_tensor("wv", [D, D], F32, kind="ExternalInput").ap()
    wod = nc.dram_tensor("wo", [D, D], F32, kind="ExternalInput").ap()
    cent = nc.dram_tensor("cent", [D, 8], F32, kind="ExternalInput").ap()
    rbias = nc.dram_tensor("rbias", [128, 128], F32, kind="ExternalInput").ap()
    wfc = nc.dram_tensor("wfc", [NE_R + NE_S, D, F_FFN], F32, kind="ExternalInput").ap()
    wpr = nc.dram_tensor("wpr", [NE_R + NE_S, F_FFN, D], F32, kind="ExternalInput").ap()
    mask01 = nc.dram_tensor("mask01", [4, 128, 128], BF16, kind="ExternalInput").ap()
    ident = nc.dram_tensor("ident", [128, 128], F32, kind="ExternalInput").ap()
    identb = nc.dram_tensor("identb", [128, 128], BF16, kind="ExternalInput").ap()
    y = nc.dram_tensor("y", [T, D], F32, kind="ExternalOutput").ap()

    with ExitStack() as ctx:
        E = ctx.enter_context
        P = Plan(nc, ctx)
        block = E(nc.Block())

        m01 = E(nc.sbuf_tensor("m01", [128, 4 * 128], BF16))
        idn = E(nc.sbuf_tensor("idn", [128, 128], F32))
        idnb = E(nc.sbuf_tensor("idnb", [128, 128], BF16))
        ones1 = E(nc.sbuf_tensor("ones1", [1, 128], F32))
        smallA = E(nc.sbuf_tensor("smallA", [128, 64], F32))
        dummy = E(nc.sbuf_tensor("dummy_tail", [1, 64], F32))
        s1 = ExitStack()
        aall = s1.enter_context(nc.sbuf_tensor("aall", [128, 8 * T], BF16, side="right"))

        strip = smallA[:, 0:16]
        negm = smallA[:, 16:32]
        st_s1 = smallA[:, 32:33]
        st_nm = smallA[:, 33:34]
        st_s2 = smallA[:, 34:35]
        st_var = smallA[:, 35:36]
        st_l = smallA[:, 36:37]
        st_rstd = smallA[:, 37:38]
        st_rstd2 = smallA[:, 38:39]
        st_eps = smallA[:, 39:40]

        def gld(dst, src, name, tag):
            P.op("gp", lambda g, dst=dst, src=src: g.dma_start(out=dst, in_=src),
                 writes=[name], tag=tag)

        def sld(dst, src, name, tag):
            P.op("sp", lambda s, dst=dst, src=src: s.dma_start(out=dst, in_=src),
                 writes=[name], tag=tag)

        sld(m01[:].rearrange("p (b c) -> p b c", b=4),
            mask01.rearrange("b k c -> k b c"), "m01", "ld_m01")
        sld(idn[:], ident, "idn", "ld_idn")
        sld(idnb[:], identb, "idnb", "ld_idnb")
        P.op("dve", lambda v: v.memset(ones1[:], 1.0), writes=["ones1"], tag="ones")

        with ExitStack() as actx:
            A = actx.enter_context
            qa = [A(nc.sbuf_tensor(f"qa{i}", [65, T], BF16)) for i in range(4)]
            rsig = A(nc.sbuf_tensor("rsig", [1, 512], F32))
            bcs = A(nc.sbuf_tensor("bcs", [64, 512], F32))

            ps = [A(nc.psum_tensor(f"ps{i}", [128, 512], F32)) for i in range(8)]

            wqb = A(nc.sbuf_tensor("wqb", [128, 8 * D], BF16))
            wkb = A(nc.sbuf_tensor("wkb", [128, 8 * D], BF16))
            wvb = A(nc.sbuf_tensor("wvb", [128, 8 * D], BF16))
            ka = [A(nc.sbuf_tensor(f"ka{i}", [65, S], BF16)) for i in range(3)]
            va = [A(nc.sbuf_tensor(f"va{i}", [128, KC * 65], BF16)) for i in range(3)]
            xc = [A(nc.sbuf_tensor(f"xc{i}", [128, 8 * 256], BF16)) for i in range(3)]
            eb = [A(nc.sbuf_tensor(f"eb{i}", [128, T], BF16)) for i in range(4)]
            gld(wqb[:].rearrange("p (dc c) -> p dc c", dc=8),
                wqd.rearrange("(dc p) c -> p dc c", p=128), "wqb", "ld_wq")
            gld(wkb[:].rearrange("p (dc c) -> p dc c", dc=8),
                wkd.rearrange("(dc p) c -> p dc c", p=128), "wkb", "ld_wk")
            gld(wvb[:].rearrange("p (dc c) -> p dc c", dc=8),
                wvd.rearrange("(dc p) c -> p dc c", p=128), "wvb", "ld_wv")

            def init_attn(v):
                for i in range(3):
                    v.memset(va[i][:].rearrange("p (kc c) -> p kc c", c=65)[:, :, 64:65], 1.0)
                    v.memset(ka[i][64:65, :], 1.0)
                v.memset(smallA[:], 0.0)
                return v.memset(st_eps, EPS)
            P.op("dve", init_attn, writes=["init_scratch"], tag="init_attn_a")
            P.op("dve", init_attn,
                 writes=["smallA"] + [f"va{i}" for i in range(3)] + [f"ka{i}" for i in range(3)],
                 tag="init_attn")

            rot01 = [0]

            def gen_kvq(beta):
                pq = beta
                sk = [(2 * beta) % 3, (2 * beta + 1) % 3]
                sq = [(2 * beta) % 4, (2 * beta + 1) % 4]

                def emit_load(i):
                    if i < 32:
                        buf, bufname = xc[i % 3], f"xc{i % 3}"
                        gld(buf[:].rearrange("p (dc c) -> p dc c", dc=8),
                            xt.rearrange("(dc p) t -> p dc t", p=128)
                            [:, :, i * 256:(i + 1) * 256],
                            bufname, f"ld_xc{beta}.{i}")
                    elif i < 40:
                        qc = i - 32
                        buf, bufname = xc[i % 3], f"xc{i % 3}"
                        gld(buf[:].rearrange("p (dc c) -> p dc c", dc=8),
                            xto.rearrange("(dc p) t -> p dc t", p=128)
                            [:, :, qc * 256:(qc + 1) * 256],
                            bufname, f"ld_xq{beta}.{qc}")
                    else:
                        P.op("gp", lambda g: g.dma_start(
                            out=dummy[0:1, 0:32], in_=ident[0:1, 0:32]),
                            writes=["dummy_tail"], tag=f"gp_pad{beta}")

                first = [True]
                for ck in range(32):
                    def unit(ck=ck):
                        if first[0]:
                            emit_load(0)
                            first[0] = False
                        emit_load(ck + 1)
                        buf = xc[ck % 3]
                        bufname = f"xc{ck % 3}"
                        bank = ps[rot01[0] % 2]
                        bname = f"ps{rot01[0] % 2}"
                        rot01[0] += 1

                        def pe_k(t, bank=bank, buf=buf):
                            for dc in range(8):
                                ins = t.matmul(
                                    bank[:, 0:256],
                                    wkb[:, dc * D + pq * 128: dc * D + pq * 128 + 128],
                                    buf[:, dc * 256:(dc + 1) * 256],
                                    start=(dc == 0), stop=(dc == 7))
                            return ins
                        P.op("pe", pe_k, reads=["wkb", bufname], writes=[bname],
                             tag=f"Kmm{beta}.{ck}")

                        def a_k(sc, bank=bank, ck=ck):
                            sc.activation(ka[sk[0]][0:64, ck * 256:(ck + 1) * 256],
                                          bank[0:64, 0:256], AF.Copy)
                            return sc.activation(
                                ka[sk[1]][0:64, ck * 256:(ck + 1) * 256],
                                bank[64:128, 0:256], AF.Copy)
                        P.op("act", a_k, reads=[bname],
                             writes=[f"ka{sk[0]}", f"ka{sk[1]}"], tag=f"Kev{beta}.{ck}")

                        bank = ps[rot01[0] % 2]
                        bname = f"ps{rot01[0] % 2}"
                        rot01[0] += 1

                        def pe_v(t, bank=bank, buf=buf):
                            ins = None
                            for sub in range(2):
                                for dc in range(8):
                                    ins = t.matmul(
                                        bank[:, sub * 128:(sub + 1) * 128],
                                        buf[:, dc * 256 + sub * 128: dc * 256 + sub * 128 + 128],
                                        wvb[:, dc * D + pq * 128: dc * D + pq * 128 + 128],
                                        start=(sub == 0 and dc == 0),
                                        stop=(sub == 1 and dc == 7))
                            return ins
                        P.op("pe", pe_v, reads=["wvb", bufname], writes=[bname],
                             tag=f"Vmm{beta}.{ck}")

                        def a_v(sc, bank=bank, ck=ck):
                            ins = None
                            for sub in range(2):
                                kc = ck * 2 + sub
                                for i in range(2):
                                    ins = sc.activation(
                                        va[sk[i]][:, kc * 65: kc * 65 + 64],
                                        bank[:, sub * 128 + i * 64: sub * 128 + i * 64 + 64],
                                        AF.Copy)
                            return ins
                        P.op("act", a_v, reads=[bname],
                             writes=[f"va{sk[0]}", f"va{sk[1]}"], tag=f"Vev{beta}.{ck}")
                    yield unit
                for qc in range(8):
                    def unit(qc=qc):
                        emit_load(32 + qc + 1)
                        buf = xc[(32 + qc) % 3]
                        bufname = f"xc{(32 + qc) % 3}"
                        bank = ps[rot01[0] % 2]
                        bname = f"ps{rot01[0] % 2}"
                        rot01[0] += 1

                        def pe_q(t, bank=bank, buf=buf):
                            for dc in range(8):
                                ins = t.matmul(
                                    bank[:, 0:256],
                                    wqb[:, dc * D + pq * 128: dc * D + pq * 128 + 128],
                                    buf[:, dc * 256:(dc + 1) * 256],
                                    start=(dc == 0), stop=(dc == 7))
                            return ins
                        P.op("pe", pe_q, reads=["wqb", bufname], writes=[bname],
                             tag=f"Qmm{beta}.{qc}")

                        def a_q(sc, bank=bank, qc=qc):
                            sc.activation(qa[sq[0]][0:64, qc * 256:(qc + 1) * 256],
                                          bank[0:64, 0:256], AF.Copy, scale=0.125)
                            return sc.activation(
                                qa[sq[1]][0:64, qc * 256:(qc + 1) * 256],
                                bank[64:128, 0:256], AF.Copy, scale=0.125)
                        P.op("act", a_q, reads=[bname],
                             writes=[f"qa{sq[0]}", f"qa{sq[1]}"], tag=f"Qev{beta}.{qc}")
                    yield unit

            def gen_pass1(h):
                ks = h % 3
                qs = h % 4
                kah, qah = f"ka{ks}", f"qa{qs}"
                for j in range(NTILE):
                    for c in range(j + 1):
                        bank = ps[rot01[0] % 2]
                        bname = f"ps{rot01[0] % 2}"
                        rot01[0] += 1

                        def p1mm(t, bank=bank, h=h, j=j, c=c, ks=ks, qs=qs):
                            return t.matmul(bank[:],
                                            qa[qs][0:64, j * 128:(j + 1) * 128],
                                            ka[ks][0:64, c * 512:(c + 1) * 512],
                                            start=True, stop=True)

                        def p1red(v, bank=bank, c=c):
                            return v.reduce_max(strip[:, c:c + 1], bank[:], axis=AX.X)

                        def unit(bank=bank, bname=bname, p1mm=p1mm, p1red=p1red,
                                 h=h, j=j, c=c):
                            P.op("pe", p1mm, reads=[kah, qah], writes=[bname],
                                 tag=f"p1mm{h}.{j}.{c}")
                            P.op("dve", p1red, reads=[bname], writes=["strip"],
                                 tag=f"p1red{h}.{j}.{c}")
                        yield unit

                    def extr_a(h=h, j=j, n=j + 1):
                        def sred(v, j=j, n=n):
                            return v.reduce_max(negm[:, j:j + 1], strip[:, 0:n],
                                                axis=AX.X, negate=True)
                        P.op("dve", sred, reads=["strip"], writes=["negm"],
                             force_sync=True, tag=f"sred{h}.{j}")
                    yield extr_a

                    def extr_b(h=h, j=j, qs=qs, qah=qah):
                        bank = ps[rot01[0] % 2]
                        bname = f"ps{rot01[0] % 2}"
                        rot01[0] += 1

                        def mtr(t, bank=bank, j=j):
                            return t.transpose(bank[0:1, 0:128], negm[:, j:j + 1], idn[:])
                        P.op("pe", mtr, reads=["negm", "idn"], writes=[bname],
                             tag=f"mtr{h}.{j}")

                        def mcp(sc, bank=bank, qs=qs, j=j):
                            return sc.activation(qa[qs][64:65, j * 128:(j + 1) * 128],
                                                 bank[0:1, 0:128], AF.Copy)
                        P.op("act", mcp, reads=[bname], writes=[qah], tag=f"mcp{h}.{j}")
                    yield extr_b

            def pieces_for(kc):
                c0 = (kc // 4) * 128
                return [(p, max(c0, p * 512), min(T, (p + 1) * 512))
                        for p in range(4) if (p + 1) * 512 > c0]

            def gen_pass2(h):
                ks = h % 3
                qs = h % 4
                kah, qah, vah = f"ka{ks}", f"qa{qs}", f"va{ks}"
                rot = [0]
                for kc in range(KC):
                    ebuf = eb[kc % 4]
                    ebname = f"eb{kc % 4}"
                    c0 = (kc // 4) * 128
                    pcs = pieces_for(kc)

                    def unit_a(h=h, kc=kc, ebuf=ebuf, ebname=ebname, pcs=pcs,
                               ks=ks, qs=qs, kah=kah, qah=qah, rot=rot):
                        for (p, a, b) in pcs:
                            bank = ps[2 + rot[0] % 2]
                            bname = f"ps{2 + rot[0] % 2}"
                            rot[0] += 1

                            def p2mm(t, bank=bank, kc=kc, a=a, b=b):
                                return t.matmul(bank[:, 0:b - a],
                                                ka[ks][0:65, kc * 128:(kc + 1) * 128],
                                                qa[qs][0:65, a:b], start=True, stop=True)
                            P.op("pe", p2mm, reads=[kah, qah], writes=[bname],
                                 tag=f"p2mm{h}.{kc}.{p}")

                            def aexp(sc, bank=bank, a=a, b=b):
                                return sc.activation(ebuf[:, a:b], bank[:, 0:b - a],
                                                     AF.Exp)
                            P.op("act", aexp, reads=[bname], writes=[ebname],
                                 tag=f"exp{h}.{kc}.{p}")
                    yield unit_a

                    def unit_b(h=h, kc=kc, ebuf=ebuf, ebname=ebname, c0=c0, pcs=pcs,
                               ks=ks, vah=vah):
                        def msk(v, kc=kc, c0=c0):
                            return v.tensor_tensor(
                                ebuf[:, c0:c0 + 128], ebuf[:, c0:c0 + 128],
                                m01[:, (kc % 4) * 128:(kc % 4 + 1) * 128], ALU.mult)
                        P.op("dve", msk, reads=[ebname, "m01"], writes=[ebname],
                             tag=f"msk{h}.{kc}")

                        for (p, a, b) in pcs:
                            pvname = f"ps{4 + p}"

                            def pv(t, kc=kc, p=p, a=a, b=b):
                                return t.matmul(
                                    ps[4 + p][0:65, a - p * 512:b - p * 512],
                                    va[ks][:, kc * 65:(kc + 1) * 65],
                                    ebuf[:, a:b],
                                    start=(kc == 0), stop=(kc == 16 * p + 15))
                            P.op("pe", pv, reads=[vah, ebname], writes=[pvname],
                                 tag=f"pv{h}.{kc}.{p}")
                    yield unit_b

            def gen_norm(h):
                for p in range(4):
                    def unit_rcp(h=h, p=p):
                        def rcp(v, p=p):
                            return v.reciprocal(rsig[0:1, 0:512],
                                                ps[4 + p][64:65, 0:512])
                        P.op("dve", rcp, reads=[f"ps{4 + p}"], writes=["rsig"],
                             tag=f"rcp{h}.{p}")
                    yield unit_rcp

                    def unit_nrm(h=h, p=p):
                        pvname = f"ps{4 + p}"

                        def bcmm(t, p=p):
                            return t.matmul(ps[2][0:64, 0:512], ones1[0:1, 0:64],
                                            rsig[0:1, 0:512], start=True, stop=True)
                        P.op("pe", bcmm, reads=["ones1", "rsig"], writes=["ps2"],
                             tag=f"bcmm{h}.{p}")

                        def bcev(v):
                            return v.tensor_copy(bcs[:], ps[2][0:64, 0:512])
                        P.op("dve", bcev, reads=["ps2"], writes=["bcs"],
                             tag=f"bcev{h}.{p}")

                        def nrm(v, h=h, p=p):
                            return v.tensor_tensor(
                                aall[(h % 2) * 64:(h % 2) * 64 + 64,
                                     (h // 2) * T + p * 512:(h // 2) * T + p * 512 + 512],
                                ps[4 + p][0:64, 0:512], bcs[:], ALU.mult)
                        P.op("dve", nrm, reads=[pvname, "bcs"], writes=["aall"],
                             tag=f"nrm{h}.{p}")
                    yield unit_nrm

            def drain(gen):
                for u in gen:
                    u()

            def chain(*gens):
                for g in gens:
                    for u in g:
                        yield u

            def merge(g1, g2, ratio=2):
                done1 = done2 = False
                it1, it2 = iter(g1), iter(g2)
                while not (done1 and done2):
                    if not done1:
                        try:
                            next(it1)()
                        except StopIteration:
                            done1 = True
                    if not done2:
                        for _ in range(ratio):
                            try:
                                next(it2)()
                            except StopIteration:
                                done2 = True
                                break

            drain(gen_kvq(0))
            drain(gen_pass1(0))
            for h in range(NHEADS):
                g2 = gen_pass2(h) if h == 0 else chain(gen_norm(h - 1), gen_pass2(h))
                if h % 2 == 0:
                    merge(g2, gen_pass1(h + 1), ratio=2)
                else:
                    beta = h // 2 + 1
                    if beta < NHEADS // 2:
                        nxt = chain(gen_kvq(beta), gen_pass1(h + 1))
                        merge(g2, nxt, ratio=2)
                    else:
                        drain(g2)
            drain(gen_norm(NHEADS - 1))

        P.fence()
        x1bf = E(nc.sbuf_tensor("x1bf", [128, 8 * T], BF16))
        out_acc = E(nc.sbuf_tensor("out_acc", [128, NTILE * D], F32))
        combine = E(nc.sbuf_tensor("combine", [128, 128], F32))
        wob = s1.enter_context(nc.sbuf_tensor("wob", [128, 8 * D], BF16, side="right"))
        gld(wob[:].rearrange("p (hc d) -> p hc d", hc=8),
            wod.rearrange("(hc p) d -> p hc d", p=128), "wob", "ld_wo")
        P.op("gp", lambda g: g.dma_start(out=dummy[0:1, 0:32], in_=ident[0:1, 0:32]),
             writes=["dummy_tail"], tag="gp_pad_wo")
        with ExitStack() as wctx:
            W = wctx.enter_context
            xot = [W(nc.sbuf_tensor(f"xot{i}", [128, D], F32)) for i in range(3)]
            t1 = [W(nc.sbuf_tensor(f"t1_{i}", [128, D], F32)) for i in range(2)]
            x1tmb_ = [W(nc.sbuf_tensor(f"x1tmb{i}", [128, D], BF16)) for i in range(2)]
            psw = [W(nc.psum_tensor(f"psw{i}", [128, 512], F32)) for i in range(4)]
            psb = [W(nc.psum_tensor(f"psb{i}", [128, 512], BF16)) for i in range(2)]

            def ld_xot(tt):
                sld(xot[tt % 3][:],
                    xtm.rearrange("(j p) d -> p j d", p=128)[:, tt, :],
                    f"xot{tt % 3}", f"ld_xo{tt}")

            ld_xot(0)
            ld_xot(1)
            for tt in range(NTILE):
                if tt + 2 < NTILE:
                    ld_xot(tt + 2)
                elif tt == NTILE - 2:
                    P.op("sp", lambda sx: sx.dma_start(
                        out=dummy[0:1, 32:64], in_=ident[0:1, 32:64]),
                        writes=["dummy_sp"], tag="sp_pad_wo")
                xbuf = xot[tt % 3]
                xbname = f"xot{tt % 3}"
                tbuf = t1[tt % 2]
                tbname = f"t1_{tt % 2}"
                for dh in range(2):
                    bank = psw[(2 * tt + dh) % 4]
                    bname = f"psw{(2 * tt + dh) % 4}"

                    def womm(t, bank=bank, tt=tt, dh=dh):
                        for hc in range(8):
                            ins = t.matmul(
                                bank[:],
                                aall[:, hc * T + tt * 128: hc * T + tt * 128 + 128],
                                wob[:, hc * D + dh * 512: hc * D + dh * 512 + 512],
                                start=(hc == 0), stop=(hc == 7))
                        return ins
                    P.op("pe", womm, reads=["aall", "wob"], writes=[bname],
                         tag=f"womm{tt}.{dh}")

                    def wadd(v, bank=bank, tbuf=tbuf, xbuf=xbuf, dh=dh):
                        return v.tensor_tensor(tbuf[:, dh * 512:(dh + 1) * 512],
                                               bank[:], xbuf[:, dh * 512:(dh + 1) * 512],
                                               ALU.add)
                    P.op("dve", wadd, reads=[bname, xbname], writes=[tbname],
                         tag=f"wadd{tt}.{dh}")

                sc_buf = t1[(tt + 1) % 2]
                sc_name = f"t1_{(tt + 1) % 2}"

                P.op("dve", lambda v, tbuf=tbuf: v.reduce_sum(st_s1, tbuf[:], axis=AX.X),
                     reads=[tbname], writes=["smallA"], force_sync=True, tag=f"ln1s1_{tt}")
                P.op("dve", lambda v: v.tensor_scalar(st_nm, st_s1, -1.0 / D, None,
                                                      ALU.mult),
                     reads=["smallA"], writes=["smallA"], force_sync=True,
                     tag=f"ln1nm_{tt}")
                P.op("dve", lambda v, tbuf=tbuf, sc_buf=sc_buf: v.tensor_scalar(
                    sc_buf[:], tbuf[:], st_nm, None, ALU.add),
                    reads=[tbname, "smallA"], writes=[sc_name], force_sync=True,
                    tag=f"ln1ctr_{tt}")
                P.op("dve", lambda v, sc_buf=sc_buf: v.tensor_tensor(
                    sc_buf[:], sc_buf[:], sc_buf[:], ALU.mult),
                    reads=[sc_name], writes=[sc_name], force_sync=True,
                    tag=f"ln1sq_{tt}")
                P.op("dve", lambda v, sc_buf=sc_buf: v.reduce_sum(
                    st_s2, sc_buf[:], axis=AX.X),
                    reads=[sc_name], writes=["smallA"], force_sync=True,
                    tag=f"ln1s2_{tt}")
                P.op("dve", lambda v: v.tensor_scalar(st_var, st_s2, 1.0 / D, None,
                                                      ALU.mult),
                     reads=["smallA"], writes=["smallA"], force_sync=True,
                     tag=f"ln1var_{tt}")
                P.op("act", lambda sc: sc.activation(st_l, st_var, AF.Ln, bias=st_eps),
                     reads=["smallA"], writes=["smallA"], force_sync=True,
                     tag=f"ln1l_{tt}")
                P.op("act", lambda sc: sc.activation(st_rstd, st_l, AF.Exp, scale=-0.5),
                     reads=["smallA"], writes=["smallA"], force_sync=True,
                     tag=f"ln1r_{tt}")
                P.op("dve", lambda v: v.tensor_scalar(st_rstd2, st_rstd, 2.0, None,
                                                      ALU.mult),
                     reads=["smallA"], writes=["smallA"], force_sync=True,
                     tag=f"ln1r2_{tt}")
                P.op("dve", lambda v, tbuf=tbuf, tt=tt: v.tensor_scalar(
                    out_acc[:, tt * D:(tt + 1) * D], tbuf[:], st_nm, st_rstd2,
                    ALU.add, ALU.mult),
                    reads=[tbname, "smallA"], writes=["out_acc"], force_sync=True,
                    tag=f"ln1acc_{tt}")
                x1tmb = x1tmb_[tt % 2]
                x1name = f"x1tmb{tt % 2}"
                P.op("dve", lambda v, tbuf=tbuf, x1tmb=x1tmb: v.tensor_scalar(
                    x1tmb[:], tbuf[:], st_nm, st_rstd, ALU.add, ALU.mult),
                    reads=[tbname, "smallA"], writes=[x1name], force_sync=True,
                    tag=f"ln1x1_{tt}")
                for half in range(2):
                    bank = psb[half]
                    bname = f"psb{half}"

                    def xtr(t, bank=bank, half=half, x1tmb=x1tmb):
                        for q in range(4):
                            ins = t.transpose(
                                bank[:, q * 128:(q + 1) * 128],
                                x1tmb[:, half * 512 + q * 128: half * 512 + (q + 1) * 128],
                                idnb[:])
                        return ins
                    P.op("pe", xtr, reads=[x1name, "idnb"], writes=[bname],
                         tag=f"xtr{tt}.{half}")

                    def xev(sc, bank=bank, tt=tt, half=half):
                        return sc.activation(
                            x1bf[:].rearrange("p (dc t) -> p dc t", dc=8)
                            [:, half * 4:(half + 1) * 4, tt * 128:(tt + 1) * 128],
                            bank[:].rearrange("p (q c) -> p q c", q=4),
                            AF.Copy)
                    P.op("act", xev, reads=[bname], writes=["x1bf"],
                         tag=f"xev{tt}.{half}")

        s1.close()
        P.fence()
        with ExitStack() as rctx:
            Rr = rctx.enter_context
            centb = Rr(nc.sbuf_tensor("centb", [128, 8 * 8], BF16))
            rbs = Rr(nc.sbuf_tensor("rbs", [128, 128], F32))
            rtmp = Rr(nc.sbuf_tensor("rtmp", [128, 6 * 128 + 64], F32))
            psr = Rr(nc.psum_tensor("psr", [128, 512], F32))

            bal = rtmp[:, 0:128]
            scr = rtmp[:, 128:256]
            rep = rtmp[:, 256:384]
            qtmp = rtmp[:, 384:512]
            msk1 = rtmp[:, 512:640]
            msk2 = rtmp[:, 640:768]
            g1 = rtmp[:, 768:784]
            g2 = rtmp[:, 784:800]
            w1 = rtmp[:, 800:816]
            w2 = rtmp[:, 816:832]

            gld(centb[:].rearrange("p (dc c) -> p dc c", dc=8),
                cent.rearrange("(dc p) c -> p dc c", p=128), "centb", "ld_cent")
            P.op("gp", lambda g: g.dma_start(out=dummy[0:1, 0:32], in_=ident[0:1, 0:32]),
                 writes=["dummy_tail"], tag="gp_pad_rt")
            sld(rbs[:], rbias, "rbs", "ld_rbias")
            P.op("sp", lambda sx: sx.dma_start(
                out=dummy[0:1, 32:64], in_=ident[0:1, 32:64]),
                writes=["dummy_sp"], tag="sp_pad_rt")

            for tt in range(NTILE):
                def rmm(t, tt=tt):
                    for dc in range(8):
                        ins = t.matmul(
                            psr[:, tt * 8:(tt + 1) * 8],
                            x1bf[:, dc * T + tt * 128: dc * T + tt * 128 + 128],
                            centb[:, dc * 8:(dc + 1) * 8],
                            start=(tt == 0 and dc == 0),
                            stop=(tt == NTILE - 1 and dc == 7))
                    return ins
                P.op("pe", rmm, reads=["x1bf", "centb"], writes=["psr"],
                     tag=f"rmm{tt}")
            P.op("act", lambda sc: sc.activation(scr, psr[:, 0:128], AF.Sigmoid),
                 reads=["psr"], writes=["rtmp"], tag="sig_scr")
            P.op("dve", lambda v: v.tensor_tensor(qtmp, psr[:, 0:128], rbs[:],
                                                  ALU.add),
                 reads=["psr", "rbs"], writes=["rtmp"], tag="rawb")
            P.op("act", lambda sc: sc.activation(bal, qtmp, AF.Sigmoid),
                 reads=["rtmp"], writes=["rtmp"], force_sync=True, tag="sig_bal")

            def seg(ap):
                return ap.rearrange("p (s e) -> p s e", e=8)

            def col3(ap):
                return ap.rearrange("p (s one) -> p s one", one=1)

            def dop(fn, tag):
                P.op("dve", fn, reads=["rtmp"], writes=["rtmp"], force_sync=True,
                     tag=tag)

            dop(lambda v: v.reduce_max(col3(g1), seg(bal), axis=AX.X), "r_m1")
            for e in range(8):
                dop(lambda v, e=e: v.tensor_copy(
                    seg(rep)[:, :, e:e + 1], col3(g1)), f"r_rep1{e}")
            dop(lambda v: v.tensor_tensor(msk1, bal, rep, ALU.is_ge), "r_msk1")
            dop(lambda v: v.scalar_tensor_tensor(qtmp, msk1, -1e30, bal,
                                                 ALU.mult, ALU.add), "r_balx")
            dop(lambda v: v.reduce_max(col3(g2), seg(qtmp), axis=AX.X), "r_m2")
            for e in range(8):
                dop(lambda v, e=e: v.tensor_copy(
                    seg(rep)[:, :, e:e + 1], col3(g2)), f"r_rep2{e}")
            dop(lambda v: v.tensor_tensor(msk2, qtmp, rep, ALU.is_ge), "r_msk2")
            dop(lambda v: v.tensor_tensor(qtmp, msk1, scr, ALU.mult), "r_q1")
            dop(lambda v: v.reduce_sum(col3(g1), seg(qtmp), axis=AX.X), "r_g1")
            dop(lambda v: v.tensor_tensor(qtmp, msk2, scr, ALU.mult), "r_q2")
            dop(lambda v: v.reduce_sum(col3(g2), seg(qtmp), axis=AX.X), "r_g2")
            dop(lambda v: v.tensor_tensor(w1, g1, g2, ALU.subtract), "r_d1")
            P.op("act", lambda sc: sc.activation(w2, w1, AF.Sigmoid, scale=-1.0),
                 reads=["rtmp"], writes=["rtmp"], force_sync=True, tag="r_w2")
            P.op("act", lambda sc: sc.activation(w1, w1, AF.Sigmoid),
                 reads=["rtmp"], writes=["rtmp"], force_sync=True, tag="r_w1")
            for e in range(8):
                dop(lambda v, e=e: v.tensor_copy(
                    seg(rep)[:, :, e:e + 1], col3(w1)), f"r_repw1{e}")
            dop(lambda v: v.tensor_tensor(qtmp, msk1, rep, ALU.mult), "r_c1")
            for e in range(8):
                dop(lambda v, e=e: v.tensor_copy(
                    seg(rep)[:, :, e:e + 1], col3(w2)), f"r_repw2{e}")
            dop(lambda v: v.tensor_tensor(rep, msk2, rep, ALU.mult), "r_c2")
            P.op("dve", lambda v: v.tensor_tensor(combine[:], qtmp, rep, ALU.add),
                 reads=["rtmp"], writes=["combine"], force_sync=True, tag="r_comb")

        P.fence()
        with ExitStack() as fctx:
            Ff = fctx.enter_context
            wfcb = [Ff(nc.sbuf_tensor(f"wfcb{i}", [128, 8 * 1024], BF16))
                    for i in range(2)]
            wprb = [Ff(nc.sbuf_tensor(f"wprb{i}", [128, 8 * 1024], BF16))
                    for i in range(2)]
            hb = Ff(nc.sbuf_tensor("hb", [128, 8 * T], BF16))
            psf = [Ff(nc.psum_tensor(f"psf{i}", [128, 512], F32)) for i in range(2)]
            psp = [Ff(nc.psum_tensor(f"psp{i}", [128, 512], F32)) for i in range(3)]

            NUNIT = 4
            units = [(e, u) for e in range(NE_R + NE_S) for u in range(NUNIT)]

            def load_unit(idx):
                e, u = units[idx]
                slot = idx % 2
                gld(wfcb[slot][:].rearrange("p (dc f) -> p dc f", dc=8),
                    wfc[e].rearrange("(dc p) f -> p dc f", p=128)
                    [:, :, u * 1024:(u + 1) * 1024],
                    f"wfcb{slot}", f"ldfc{idx}")
                gld(wprb[slot][:].rearrange("p (fc d) -> p fc d", fc=8),
                    wpr[e].rearrange("(fc p) d -> p fc d", p=128)
                    [:, u * 8:(u + 1) * 8, :],
                    f"wprb{slot}", f"ldpr{idx}")

            load_unit(0)
            load_unit(1)
            fcg = [0]
            prg = [0]
            for idx, (e, u) in enumerate(units):
                slot = idx % 2
                wf, wp = wfcb[slot], wprb[slot]
                wfn, wpn = f"wfcb{slot}", f"wprb{slot}"
                for Ft in range(8):
                    for tc in range(4):
                        bank = psf[fcg[0] % 2]
                        bname = f"psf{fcg[0] % 2}"
                        fcg[0] += 1

                        def fcmm(t, bank=bank, wf=wf, Ft=Ft, tc=tc):
                            for dc in range(8):
                                ins = t.matmul(
                                    bank[:],
                                    wf[:, dc * 1024 + Ft * 128: dc * 1024 + (Ft + 1) * 128],
                                    x1bf[:, dc * T + tc * 512: dc * T + tc * 512 + 512],
                                    start=(dc == 0), stop=(dc == 7))
                            return ins
                        P.op("pe", fcmm, reads=[wfn, "x1bf"], writes=[bname],
                             tag=f"fc{idx}.{Ft}.{tc}")

                        def gel(sc, bank=bank, Ft=Ft, tc=tc):
                            return sc.activation(
                                hb[:, Ft * T + tc * 512: Ft * T + tc * 512 + 512],
                                bank[:], AF.Gelu)
                        P.op("act", gel, reads=[bname], writes=["hb"],
                             tag=f"gel{idx}.{Ft}.{tc}")
                for tt in range(NTILE):
                    for dh in range(2):
                        bank = psp[prg[0] % 3]
                        bname = f"psp{prg[0] % 3}"
                        prg[0] += 1

                        def prmm(t, bank=bank, wp=wp, tt=tt, dh=dh):
                            for fc_ in range(8):
                                ins = t.matmul(
                                    bank[:],
                                    hb[:, fc_ * T + tt * 128: fc_ * T + tt * 128 + 128],
                                    wp[:, fc_ * 1024 + dh * 512: fc_ * 1024 + dh * 512 + 512],
                                    start=(fc_ == 0), stop=(fc_ == 7))
                            return ins
                        P.op("pe", prmm, reads=["hb", wpn], writes=[bname],
                             tag=f"pr{idx}.{tt}.{dh}")

                        if e < NE_R:
                            def ev(v, bank=bank, tt=tt, dh=dh, e=e):
                                dst = out_acc[:, tt * D + dh * 512:
                                              tt * D + dh * 512 + 512]
                                return v.scalar_tensor_tensor(
                                    dst, bank[:], combine[:, tt * 8 + e:tt * 8 + e + 1],
                                    dst, ALU.mult, ALU.add)
                            P.op("dve", ev, reads=[bname, "combine", "out_acc"],
                                 writes=["out_acc"], tag=f"ev{idx}.{tt}.{dh}")
                        else:
                            def ev(v, bank=bank, tt=tt, dh=dh):
                                dst = out_acc[:, tt * D + dh * 512:
                                              tt * D + dh * 512 + 512]
                                return v.tensor_tensor(dst, bank[:], dst, ALU.add)
                            P.op("dve", ev, reads=[bname, "out_acc"],
                                 writes=["out_acc"], tag=f"ev{idx}.{tt}.{dh}")
                if idx + 2 < len(units):
                    load_unit(idx + 2)
                    if idx + 2 == len(units) - 1:
                        P.op("gp", lambda g: g.dma_start(
                            out=dummy[0:1, 0:32], in_=ident[0:1, 0:32]),
                            writes=["dummy_tail"], tag="gp_pad_ffn")

        P.fence()
        with ExitStack() as lctx:
            L = lctx.enter_context
            ysb = [L(nc.sbuf_tensor(f"ysb{i}", [128, D], F32)) for i in range(2)]
            sqb = L(nc.sbuf_tensor("sqb", [128, D], F32))

            for tt in range(NTILE):
                src = out_acc[:, tt * D:(tt + 1) * D]
                P.op("dve", lambda v, src=src: v.reduce_sum(st_s1, src, axis=AX.X),
                     reads=["out_acc"], writes=["smallA"], force_sync=True,
                     tag=f"ln2s1_{tt}")
                P.op("dve", lambda v: v.tensor_scalar(st_nm, st_s1, -1.0 / D, None,
                                                      ALU.mult),
                     reads=["smallA"], writes=["smallA"], force_sync=True,
                     tag=f"ln2nm_{tt}")
                P.op("dve", lambda v, src=src: v.tensor_scalar(
                    sqb[:], src, st_nm, None, ALU.add),
                    reads=["out_acc", "smallA"], writes=["sqb"], force_sync=True,
                    tag=f"ln2ctr_{tt}")
                P.op("dve", lambda v: v.tensor_tensor(sqb[:], sqb[:], sqb[:],
                                                      ALU.mult),
                     reads=["sqb"], writes=["sqb"], force_sync=True,
                     tag=f"ln2sq_{tt}")
                P.op("dve", lambda v: v.reduce_sum(st_s2, sqb[:], axis=AX.X),
                     reads=["sqb"], writes=["smallA"], force_sync=True,
                     tag=f"ln2s2_{tt}")
                P.op("dve", lambda v: v.tensor_scalar(st_var, st_s2, 1.0 / D, None,
                                                      ALU.mult),
                     reads=["smallA"], writes=["smallA"], force_sync=True,
                     tag=f"ln2var_{tt}")
                P.op("act", lambda sc: sc.activation(st_l, st_var, AF.Ln, bias=st_eps),
                     reads=["smallA"], writes=["smallA"], force_sync=True,
                     tag=f"ln2l_{tt}")
                P.op("act", lambda sc: sc.activation(st_rstd, st_l, AF.Exp, scale=-0.5),
                     reads=["smallA"], writes=["smallA"], force_sync=True,
                     tag=f"ln2r_{tt}")
                ybuf = ysb[tt % 2]
                ybname = f"ysb{tt % 2}"
                P.op("dve", lambda v, src=src, ybuf=ybuf: v.tensor_scalar(
                    ybuf[:], src, st_nm, st_rstd, ALU.add, ALU.mult),
                    reads=["out_acc", "smallA"], writes=[ybname], force_sync=True,
                    tag=f"ln2y_{tt}")
                P.op("sp", lambda s, ybuf=ybuf, tt=tt: s.dma_start(
                    out=y.rearrange("(j p) d -> p j d", p=128)[:, tt, :],
                    in_=ybuf[:]),
                    reads=[ybname], writes=["y"], tag=f"st{tt}")

        sp_final = P.counts["sp"]

        @block.tensor
        def _(t):
            P.emit_engine("pe", t)

        @block.scalar
        def _(sc):
            P.emit_engine("act", sc)

        @block.vector
        def _(v):
            P.emit_engine("dve", v)

        @block.sync
        def _(s):
            P.emit_engine("sp", s)
            s.wait_ge(P.sems["sp"], sp_final)

        @block.gpsimd
        def _(g):
            P.emit_engine("gp", g)

    return nc


_NC_CACHE = None


def kernel(L0_attn_x, L0_attn_WQ, L0_attn_WK, L0_attn_WV, L0_attn_WO,
           L0_ffn_centroids, L0_ffn_routing_bias, L0_ffn_Wfc_r, L0_ffn_Wproj_r,
           L0_ffn_Wfc_sh, L0_ffn_Wproj_sh):
    global _NC_CACHE
    x = np.asarray(L0_attn_x, np.float32)

    wq = np.ascontiguousarray(np.asarray(L0_attn_WQ, np.float32)
                              .transpose(2, 0, 1).reshape(D, D))
    wk = np.ascontiguousarray(np.asarray(L0_attn_WK, np.float32)
                              .transpose(2, 0, 1).reshape(D, D))
    wv = np.ascontiguousarray(np.asarray(L0_attn_WV, np.float32)
                              .transpose(2, 0, 1).reshape(D, D))
    wo = np.ascontiguousarray(np.asarray(L0_attn_WO, np.float32)
                              .transpose(0, 2, 1).reshape(D, D))
    cent = np.ascontiguousarray(np.asarray(L0_ffn_centroids, np.float32).T)
    rbias = np.ascontiguousarray(np.broadcast_to(
        np.tile(np.asarray(L0_ffn_routing_bias, np.float32), NTILE)[None, :],
        (128, 128)))
    wfc = np.ascontiguousarray(np.concatenate(
        [np.asarray(L0_ffn_Wfc_r, np.float32),
         np.asarray(L0_ffn_Wfc_sh, np.float32)], axis=0).transpose(0, 2, 1))
    wpr = np.ascontiguousarray(np.concatenate(
        [np.asarray(L0_ffn_Wproj_r, np.float32),
         np.asarray(L0_ffn_Wproj_sh, np.float32)], axis=0).transpose(0, 2, 1))
    ident = np.eye(128, dtype=np.float32)
    identb = np.eye(128, dtype=ml_dtypes.bfloat16)

    xts = [np.ascontiguousarray(x[b].T) for b in range(2)]

    in_maps = []
    for c in range(8):
        b, k = c // 4, c % 4
        rows = np.concatenate(
            [np.arange(128) + 512 * j + 128 * k for j in range(NTILE)])
        xtm_c = np.ascontiguousarray(x[b][rows, :])
        xt_own = np.ascontiguousarray(xts[b][:, rows])
        mask = np.zeros((4, 128, 128), np.float32)
        for bb in range(4):
            kk = np.arange(128)[:, None]
            rr = np.arange(128)[None, :]
            mask[bb] = (128 * bb + kk <= 128 * k + rr)
        in_maps.append(dict(
            xt=xts[b], xtm=xtm_c, xt_own=xt_own,
            wq=wq, wk=wk, wv=wv, wo=wo, cent=cent, rbias=rbias,
            wfc=wfc, wpr=wpr,
            mask01=mask.astype(ml_dtypes.bfloat16), ident=ident, identb=identb,
        ))

    if _NC_CACHE is None:
        _NC_CACHE = build_program()
    res = run_bass_kernel_spmd(_NC_CACHE, in_maps, core_ids=list(range(8)))

    out = np.empty((2, S, D), np.float32)
    for c in range(8):
        b, k = c // 4, c % 4
        yc = np.asarray(res.results[c]["y"], np.float32)
        for j in range(NTILE):
            out[b, 512 * j + 128 * k: 512 * j + 128 * k + 128, :] = \
                yc[j * 128:(j + 1) * 128, :]
    return out


# version 06
# speedup vs baseline: 1.0485x; 1.0485x over previous
import numpy as np
import ml_dtypes
from collections import defaultdict
from contextlib import ExitStack

import concourse.bass as bass
import concourse.mybir as mybir
from concourse.bass_utils import run_bass_kernel_spmd

F32 = mybir.dt.float32
BF16 = mybir.dt.bfloat16
AF = mybir.ActivationFunctionType
ALU = mybir.AluOpType
AX = mybir.AxisListType

S = 8192
T = 2048
D = 1024
NHEADS = 16
KC = 64
NTILE = 16
NE_R = 8
NE_S = 2
F_FFN = 4096
NEG = -3.0e38
EPS = 1e-5


class _Res:
    __slots__ = ("writer", "readers")

    def __init__(self):
        self.writer = None
        self.readers = []


class Plan:
    ENGINES = {"pe": 1, "act": 1, "dve": 1, "sp": 16, "gp": 16}

    def __init__(self, nc, ctx):
        self.nc = nc
        self.sems = {}
        self.counts = {}
        self.ops = {e: [] for e in self.ENGINES}
        self.satisfied = {e: defaultdict(int) for e in self.ENGINES}
        for e in self.ENGINES:
            self.sems[e] = ctx.enter_context(nc.semaphore(f"sem_{e}"))
            self.counts[e] = 0
        self._res = {}
        self._fence = None
        self._fence_applied = set()
        self._gseq = 0
        self._dma_seq = {"gp": {}, "sp": {}}

    def res(self, name):
        r = self._res.get(name)
        if r is None:
            r = _Res()
            self._res[name] = r
        return r

    def fence(self):
        self._fence = [(e, self.counts[e]) for e in self.ENGINES if self.counts[e]]
        self._fence_applied = set()

    def op(self, eng, emit, reads=(), writes=(), extra_waits=(), force_sync=False,
           tag=""):
        reads = [self.res(r) if isinstance(r, str) else r for r in reads]
        writes = [self.res(r) if isinstance(r, str) else r for r in writes]
        want = {}

        def need(tok):
            if tok is None:
                return
            s, c = tok
            if want.get(s, 0) < c:
                want[s] = c

        for r in reads:
            need(r.writer)
        for r in writes:
            need(r.writer)
            for t in r.readers:
                need(t)
        for t in extra_waits:
            need(t)
        if self._fence is not None and eng not in self._fence_applied:
            for t in self._fence:
                need(t)
            self._fence_applied.add(eng)
            if len(self._fence_applied) == len(self.ENGINES):
                self._fence = None

        waits = []
        sat = self.satisfied[eng]
        for s, c in sorted(want.items()):
            if s == eng and not force_sync and c <= self.counts[eng]:
                continue
            if s == eng and force_sync and c <= self.counts[eng]:
                if sat[s] >= c:
                    continue
                waits.append((s, c))
                sat[s] = c
                continue
            if sat[s] >= c:
                continue
            waits.append((s, c))
            sat[s] = c

        step = self.ENGINES[eng]
        self.counts[eng] += step
        tok = (eng, self.counts[eng])
        seq = self._gseq
        self._gseq += 1
        if step == 16:
            self._dma_seq[eng][self.counts[eng]] = seq
        self.ops[eng].append((waits, emit, step, tag, seq))
        for r in reads:
            r.readers.append(tok)
        for r in writes:
            r.writer = tok
            r.readers = []
        return tok

    def emit_engine(self, eng, obj):
        sems = self.sems
        for waits, emit, step, tag, seq in self.ops[eng]:
            for s, c in waits:
                if s in ("gp", "sp"):
                    nseq = self._dma_seq[s].get(c + 16)
                    if nseq is not None and nseq < seq:
                        c = c + 16
                obj.wait_ge(sems[s], c)
            last = emit(obj)
            assert last is not None, f"emit {tag!r} on {eng} returned None"
            last.then_inc(sems[eng], step)


def build_program():
    nc = bass.Bass("TRN2", target_bir_lowering=False, debug=False)

    xt = nc.dram_tensor("xt", [D, S], F32, kind="ExternalInput").ap()
    xto = nc.dram_tensor("xt_own", [D, T], F32, kind="ExternalInput").ap()
    xtm = nc.dram_tensor("xtm", [T, D], F32, kind="ExternalInput").ap()
    wqd = nc.dram_tensor("wq", [D, D], F32, kind="ExternalInput").ap()
    wkd = nc.dram_tensor("wk", [D, D], F32, kind="ExternalInput").ap()
    wvd = nc.dram_tensor("wv", [D, D], F32, kind="ExternalInput").ap()
    wod = nc.dram_tensor("wo", [D, D], F32, kind="ExternalInput").ap()
    cent = nc.dram_tensor("cent", [D, 8], F32, kind="ExternalInput").ap()
    rbias = nc.dram_tensor("rbias", [128, 128], F32, kind="ExternalInput").ap()
    wfc = nc.dram_tensor("wfc", [NE_R + NE_S, D, F_FFN], F32, kind="ExternalInput").ap()
    wpr = nc.dram_tensor("wpr", [NE_R + NE_S, F_FFN, D], F32, kind="ExternalInput").ap()
    mask01 = nc.dram_tensor("mask01", [4, 128, 128], BF16, kind="ExternalInput").ap()
    ident = nc.dram_tensor("ident", [128, 128], F32, kind="ExternalInput").ap()
    identb = nc.dram_tensor("identb", [128, 128], BF16, kind="ExternalInput").ap()
    y = nc.dram_tensor("y", [T, D], F32, kind="ExternalOutput").ap()
    x16 = nc.dram_tensor("x16s", [D, S], BF16, kind="Internal").ap()
    x16q = nc.dram_tensor("x16q", [D, T], BF16, kind="Internal").ap()

    with ExitStack() as ctx:
        E = ctx.enter_context
        P = Plan(nc, ctx)
        block = E(nc.Block())

        m01 = E(nc.sbuf_tensor("m01", [128, 4 * 128], BF16))
        idn = E(nc.sbuf_tensor("idn", [128, 128], F32))
        idnb = E(nc.sbuf_tensor("idnb", [128, 128], BF16))
        ones1 = E(nc.sbuf_tensor("ones1", [1, 128], F32))
        smallA = E(nc.sbuf_tensor("smallA", [128, 64], F32))
        dummy = E(nc.sbuf_tensor("dummy_tail", [1, 64], F32))
        s1 = ExitStack()
        aall = s1.enter_context(nc.sbuf_tensor("aall", [128, 8 * T], BF16, side="right"))

        strip = smallA[:, 0:16]
        negm = smallA[:, 16:32]
        st_s1 = smallA[:, 32:33]
        st_nm = smallA[:, 33:34]
        st_s2 = smallA[:, 34:35]
        st_var = smallA[:, 35:36]
        st_l = smallA[:, 36:37]
        st_rstd = smallA[:, 37:38]
        st_rstd2 = smallA[:, 38:39]
        st_eps = smallA[:, 39:40]

        def gld(dst, src, name, tag):
            P.op("gp", lambda g, dst=dst, src=src: g.dma_start(out=dst, in_=src),
                 writes=[name], tag=tag)

        def sld(dst, src, name, tag):
            P.op("sp", lambda s, dst=dst, src=src: s.dma_start(out=dst, in_=src),
                 writes=[name], tag=tag)

        sld(m01[:].rearrange("p (b c) -> p b c", b=4),
            mask01.rearrange("b k c -> k b c"), "m01", "ld_m01")
        sld(idn[:], ident, "idn", "ld_idn")
        sld(idnb[:], identb, "idnb", "ld_idnb")
        P.op("dve", lambda v: v.memset(ones1[:], 1.0), writes=["ones1"], tag="ones")

        with ExitStack() as actx:
            A = actx.enter_context
            qa = [A(nc.sbuf_tensor(f"qa{i}", [65, T], BF16)) for i in range(4)]
            rsig = A(nc.sbuf_tensor("rsig", [1, 512], F32))
            bcs = A(nc.sbuf_tensor("bcs", [64, 512], F32))

            ps = [A(nc.psum_tensor(f"ps{i}", [128, 512], F32)) for i in range(8)]

            wqb = A(nc.sbuf_tensor("wqb", [128, 8 * D], BF16))
            wkb = A(nc.sbuf_tensor("wkb", [128, 8 * D], BF16))
            wvb = A(nc.sbuf_tensor("wvb", [128, 8 * D], BF16))
            ka = [A(nc.sbuf_tensor(f"ka{i}", [65, S], BF16)) for i in range(3)]
            va = [A(nc.sbuf_tensor(f"va{i}", [128, KC * 65], BF16)) for i in range(3)]
            xc = [A(nc.sbuf_tensor(f"xc{i}", [128, 8 * 256], BF16)) for i in range(3)]
            eb = [A(nc.sbuf_tensor(f"eb{i}", [128, T], BF16)) for i in range(4)]
            gld(wqb[:].rearrange("p (dc c) -> p dc c", dc=8),
                wqd.rearrange("(dc p) c -> p dc c", p=128), "wqb", "ld_wq")
            gld(wkb[:].rearrange("p (dc c) -> p dc c", dc=8),
                wkd.rearrange("(dc p) c -> p dc c", p=128), "wkb", "ld_wk")
            gld(wvb[:].rearrange("p (dc c) -> p dc c", dc=8),
                wvd.rearrange("(dc p) c -> p dc c", p=128), "wvb", "ld_wv")

            def init_attn(v):
                for i in range(3):
                    v.memset(va[i][:].rearrange("p (kc c) -> p kc c", c=65)[:, :, 64:65], 1.0)
                    v.memset(ka[i][64:65, :], 1.0)
                v.memset(smallA[:], 0.0)
                return v.memset(st_eps, EPS)
            P.op("dve", init_attn, writes=["init_scratch"], tag="init_attn_a")
            P.op("dve", init_attn,
                 writes=["smallA"] + [f"va{i}" for i in range(3)] + [f"ka{i}" for i in range(3)],
                 tag="init_attn")

            rot01 = [0]

            def gen_kvq(beta):
                pq = beta
                sk = [(2 * beta) % 3, (2 * beta + 1) % 3]
                sq = [(2 * beta) % 4, (2 * beta + 1) % 4]

                def emit_load(i):
                    if i >= 40:
                        P.op("gp", lambda g: g.dma_start(
                            out=dummy[0:1, 0:32], in_=ident[0:1, 0:32]),
                            writes=["dummy_tail"], tag=f"gp_pad{beta}")
                        P.op("sp", lambda s: s.dma_start(
                            out=dummy[0:1, 32:64], in_=ident[0:1, 32:64]),
                            writes=["dummy_sp"], tag=f"sp_pad{beta}")
                        return
                    buf, bufname = xc[i % 3], f"xc{i % 3}"
                    if i < 32:
                        src32 = xt.rearrange("(dc p) t -> p dc t", p=128)[
                            :, :, i * 256:(i + 1) * 256]
                        s16 = x16.rearrange("(dc p) t -> p dc t", p=128)[
                            :, :, i * 256:(i + 1) * 256]
                        nm = f"x16.{i}"
                    else:
                        qc = i - 32
                        src32 = xto.rearrange("(dc p) t -> p dc t", p=128)[
                            :, :, qc * 256:(qc + 1) * 256]
                        s16 = x16q.rearrange("(dc p) t -> p dc t", p=128)[
                            :, :, qc * 256:(qc + 1) * 256]
                        nm = f"x16q.{i}"
                    bview = buf[:].rearrange("p (dc c) -> p dc c", dc=8)
                    if beta == 0:
                        gld(bview, src32, bufname, f"ld_x{beta}.{i}")
                        P.op("sp", lambda sx, s16=s16, bview=bview: sx.dma_start(
                            out=s16, in_=bview),
                            reads=[bufname], writes=[nm], tag=f"st_x16.{i}")
                    else:
                        P.op("sp", lambda sx, s16=s16, bview=bview: sx.dma_start(
                            out=bview, in_=s16),
                            reads=[nm], writes=[bufname], tag=f"ld16_{beta}.{i}")

                first = [True]
                for ck in range(32):
                    def unit(ck=ck):
                        if first[0]:
                            emit_load(0)
                            first[0] = False
                        emit_load(ck + 1)
                        buf = xc[ck % 3]
                        bufname = f"xc{ck % 3}"
                        bank = ps[rot01[0] % 2]
                        bname = f"ps{rot01[0] % 2}"
                        rot01[0] += 1

                        def pe_k(t, bank=bank, buf=buf):
                            for dc in range(8):
                                ins = t.matmul(
                                    bank[:, 0:256],
                                    wkb[:, dc * D + pq * 128: dc * D + pq * 128 + 128],
                                    buf[:, dc * 256:(dc + 1) * 256],
                                    start=(dc == 0), stop=(dc == 7))
                            return ins
                        P.op("pe", pe_k, reads=["wkb", bufname], writes=[bname],
                             tag=f"Kmm{beta}.{ck}")

                        def a_k(sc, bank=bank, ck=ck):
                            sc.activation(ka[sk[0]][0:64, ck * 256:(ck + 1) * 256],
                                          bank[0:64, 0:256], AF.Copy)
                            return sc.activation(
                                ka[sk[1]][0:64, ck * 256:(ck + 1) * 256],
                                bank[64:128, 0:256], AF.Copy)
                        P.op("act", a_k, reads=[bname],
                             writes=[f"ka{sk[0]}", f"ka{sk[1]}"], tag=f"Kev{beta}.{ck}")

                        bank = ps[rot01[0] % 2]
                        bname = f"ps{rot01[0] % 2}"
                        rot01[0] += 1

                        def pe_v(t, bank=bank, buf=buf):
                            ins = None
                            for sub in range(2):
                                for dc in range(8):
                                    ins = t.matmul(
                                        bank[:, sub * 128:(sub + 1) * 128],
                                        buf[:, dc * 256 + sub * 128: dc * 256 + sub * 128 + 128],
                                        wvb[:, dc * D + pq * 128: dc * D + pq * 128 + 128],
                                        start=(sub == 0 and dc == 0),
                                        stop=(sub == 1 and dc == 7))
                            return ins
                        P.op("pe", pe_v, reads=["wvb", bufname], writes=[bname],
                             tag=f"Vmm{beta}.{ck}")

                        def a_v(sc, bank=bank, ck=ck):
                            ins = None
                            for sub in range(2):
                                kc = ck * 2 + sub
                                for i in range(2):
                                    ins = sc.activation(
                                        va[sk[i]][:, kc * 65: kc * 65 + 64],
                                        bank[:, sub * 128 + i * 64: sub * 128 + i * 64 + 64],
                                        AF.Copy)
                            return ins
                        P.op("act", a_v, reads=[bname],
                             writes=[f"va{sk[0]}", f"va{sk[1]}"], tag=f"Vev{beta}.{ck}")
                    yield unit
                for qc in range(8):
                    def unit(qc=qc):
                        emit_load(32 + qc + 1)
                        buf = xc[(32 + qc) % 3]
                        bufname = f"xc{(32 + qc) % 3}"
                        bank = ps[rot01[0] % 2]
                        bname = f"ps{rot01[0] % 2}"
                        rot01[0] += 1

                        def pe_q(t, bank=bank, buf=buf):
                            for dc in range(8):
                                ins = t.matmul(
                                    bank[:, 0:256],
                                    wqb[:, dc * D + pq * 128: dc * D + pq * 128 + 128],
                                    buf[:, dc * 256:(dc + 1) * 256],
                                    start=(dc == 0), stop=(dc == 7))
                            return ins
                        P.op("pe", pe_q, reads=["wqb", bufname], writes=[bname],
                             tag=f"Qmm{beta}.{qc}")

                        def a_q(sc, bank=bank, qc=qc):
                            sc.activation(qa[sq[0]][0:64, qc * 256:(qc + 1) * 256],
                                          bank[0:64, 0:256], AF.Copy, scale=0.125)
                            return sc.activation(
                                qa[sq[1]][0:64, qc * 256:(qc + 1) * 256],
                                bank[64:128, 0:256], AF.Copy, scale=0.125)
                        P.op("act", a_q, reads=[bname],
                             writes=[f"qa{sq[0]}", f"qa{sq[1]}"], tag=f"Qev{beta}.{qc}")
                    yield unit

            def gen_pass1(h):
                ks = h % 3
                qs = h % 4
                kah, qah = f"ka{ks}", f"qa{qs}"
                for j in range(NTILE):
                    for c in range(j + 1):
                        bank = ps[rot01[0] % 2]
                        bname = f"ps{rot01[0] % 2}"
                        rot01[0] += 1

                        def p1mm(t, bank=bank, h=h, j=j, c=c, ks=ks, qs=qs):
                            return t.matmul(bank[:],
                                            qa[qs][0:64, j * 128:(j + 1) * 128],
                                            ka[ks][0:64, c * 512:(c + 1) * 512],
                                            start=True, stop=True)

                        def p1red(v, bank=bank, c=c):
                            return v.reduce_max(strip[:, c:c + 1], bank[:], axis=AX.X)

                        def unit(bank=bank, bname=bname, p1mm=p1mm, p1red=p1red,
                                 h=h, j=j, c=c):
                            P.op("pe", p1mm, reads=[kah, qah], writes=[bname],
                                 tag=f"p1mm{h}.{j}.{c}")
                            P.op("dve", p1red, reads=[bname], writes=["strip"],
                                 tag=f"p1red{h}.{j}.{c}")
                        yield unit

                    def extr_a(h=h, j=j, n=j + 1):
                        def sred(v, j=j, n=n):
                            return v.reduce_max(negm[:, j:j + 1], strip[:, 0:n],
                                                axis=AX.X, negate=True)
                        P.op("dve", sred, reads=["strip"], writes=["negm"],
                             force_sync=True, tag=f"sred{h}.{j}")
                    yield extr_a

                    def extr_b(h=h, j=j, qs=qs, qah=qah):
                        bank = ps[rot01[0] % 2]
                        bname = f"ps{rot01[0] % 2}"
                        rot01[0] += 1

                        def mtr(t, bank=bank, j=j):
                            return t.transpose(bank[0:1, 0:128], negm[:, j:j + 1], idn[:])
                        P.op("pe", mtr, reads=["negm", "idn"], writes=[bname],
                             tag=f"mtr{h}.{j}")

                        def mcp(sc, bank=bank, qs=qs, j=j):
                            return sc.activation(qa[qs][64:65, j * 128:(j + 1) * 128],
                                                 bank[0:1, 0:128], AF.Copy)
                        P.op("act", mcp, reads=[bname], writes=[qah], tag=f"mcp{h}.{j}")
                    yield extr_b

            def pieces_for(kc):
                c0 = (kc // 4) * 128
                return [(p, max(c0, p * 512), min(T, (p + 1) * 512))
                        for p in range(4) if (p + 1) * 512 > c0]

            def gen_pass2(h):
                ks = h % 3
                qs = h % 4
                kah, qah, vah = f"ka{ks}", f"qa{qs}", f"va{ks}"
                rot = [0]
                for kc in range(KC):
                    ebuf = eb[kc % 4]
                    ebname = f"eb{kc % 4}"
                    c0 = (kc // 4) * 128
                    pcs = pieces_for(kc)

                    def unit_a(h=h, kc=kc, ebuf=ebuf, ebname=ebname, c0=c0,
                               pcs=pcs, ks=ks, qs=qs, kah=kah, qah=qah, rot=rot):
                        for (p, a, b) in pcs:
                            bank = ps[2 + rot[0] % 2]
                            bname = f"ps{2 + rot[0] % 2}"
                            rot[0] += 1

                            def p2mm(t, bank=bank, kc=kc, a=a, b=b):
                                return t.matmul(bank[:, 0:b - a],
                                                ka[ks][0:65, kc * 128:(kc + 1) * 128],
                                                qa[qs][0:65, a:b], start=True, stop=True)
                            P.op("pe", p2mm, reads=[kah, qah], writes=[bname],
                                 tag=f"p2mm{h}.{kc}.{p}")

                            if a == c0:
                                def mska(v, bank=bank, kc=kc, a=a, c0=c0):
                                    return v.tensor_tensor(
                                        bank[:, c0 - a:c0 - a + 128],
                                        bank[:, c0 - a:c0 - a + 128],
                                        m01[:, (kc % 4) * 128:(kc % 4 + 1) * 128],
                                        ALU.add)
                                P.op("dve", mska, reads=[bname, "m01"],
                                     writes=[bname], tag=f"mska{h}.{kc}")

                            def aexp(sc, bank=bank, a=a, b=b):
                                return sc.activation(ebuf[:, a:b], bank[:, 0:b - a],
                                                     AF.Exp)
                            P.op("act", aexp, reads=[bname], writes=[ebname],
                                 tag=f"exp{h}.{kc}.{p}")
                    yield unit_a

                    def unit_b(h=h, kc=kc, ebuf=ebuf, ebname=ebname, pcs=pcs,
                               ks=ks, vah=vah):
                        for (p, a, b) in pcs:
                            pvname = f"ps{4 + p}"

                            def pv(t, kc=kc, p=p, a=a, b=b):
                                return t.matmul(
                                    ps[4 + p][0:65, a - p * 512:b - p * 512],
                                    va[ks][:, kc * 65:(kc + 1) * 65],
                                    ebuf[:, a:b],
                                    start=(kc == 0), stop=(kc == 16 * p + 15))
                            P.op("pe", pv, reads=[vah, ebname], writes=[pvname],
                                 tag=f"pv{h}.{kc}.{p}")
                    yield unit_b

            def gen_norm(h):
                for p in range(4):
                    def unit_rcp(h=h, p=p):
                        def rcp(v, p=p):
                            return v.reciprocal(rsig[0:1, 0:512],
                                                ps[4 + p][64:65, 0:512])
                        P.op("dve", rcp, reads=[f"ps{4 + p}"], writes=["rsig"],
                             tag=f"rcp{h}.{p}")
                    yield unit_rcp

                    def unit_nrm(h=h, p=p):
                        pvname = f"ps{4 + p}"

                        def bcmm(t, p=p):
                            return t.matmul(ps[2][0:64, 0:512], ones1[0:1, 0:64],
                                            rsig[0:1, 0:512], start=True, stop=True)
                        P.op("pe", bcmm, reads=["ones1", "rsig"], writes=["ps2"],
                             tag=f"bcmm{h}.{p}")

                        def bcev(v):
                            return v.tensor_copy(bcs[:], ps[2][0:64, 0:512])
                        P.op("dve", bcev, reads=["ps2"], writes=["bcs"],
                             tag=f"bcev{h}.{p}")

                        def nrm(v, h=h, p=p):
                            return v.tensor_tensor(
                                aall[(h % 2) * 64:(h % 2) * 64 + 64,
                                     (h // 2) * T + p * 512:(h // 2) * T + p * 512 + 512],
                                ps[4 + p][0:64, 0:512], bcs[:], ALU.mult)
                        P.op("dve", nrm, reads=[pvname, "bcs"], writes=["aall"],
                             tag=f"nrm{h}.{p}")
                    yield unit_nrm

            def drain(gen):
                for u in gen:
                    u()

            def chain(*gens):
                for g in gens:
                    for u in g:
                        yield u

            def merge(g1, g2, ratio=2):
                done1 = done2 = False
                it1, it2 = iter(g1), iter(g2)
                while not (done1 and done2):
                    if not done1:
                        try:
                            next(it1)()
                        except StopIteration:
                            done1 = True
                    if not done2:
                        for _ in range(ratio):
                            try:
                                next(it2)()
                            except StopIteration:
                                done2 = True
                                break

            drain(gen_kvq(0))
            drain(gen_pass1(0))
            for h in range(NHEADS):
                g2 = gen_pass2(h) if h == 0 else chain(gen_norm(h - 1), gen_pass2(h))
                if h % 2 == 0:
                    merge(g2, gen_pass1(h + 1), ratio=2)
                else:
                    beta = h // 2 + 1
                    if beta < NHEADS // 2:
                        nxt = chain(gen_kvq(beta), gen_pass1(h + 1))
                        merge(g2, nxt, ratio=2)
                    else:
                        drain(g2)
            drain(gen_norm(NHEADS - 1))

        P.fence()
        x1bf = E(nc.sbuf_tensor("x1bf", [128, 8 * T], BF16))
        out_acc = E(nc.sbuf_tensor("out_acc", [128, NTILE * D], F32))
        combine = E(nc.sbuf_tensor("combine", [128, 128], F32))
        wob = s1.enter_context(nc.sbuf_tensor("wob", [128, 8 * D], BF16, side="right"))
        gld(wob[:].rearrange("p (hc d) -> p hc d", hc=8),
            wod.rearrange("(hc p) d -> p hc d", p=128), "wob", "ld_wo")
        P.op("gp", lambda g: g.dma_start(out=dummy[0:1, 0:32], in_=ident[0:1, 0:32]),
             writes=["dummy_tail"], tag="gp_pad_wo")
        with ExitStack() as wctx:
            W = wctx.enter_context
            xot = [W(nc.sbuf_tensor(f"xot{i}", [128, D], F32)) for i in range(3)]
            t1 = [W(nc.sbuf_tensor(f"t1_{i}", [128, D], F32)) for i in range(2)]
            x1tmb_ = [W(nc.sbuf_tensor(f"x1tmb{i}", [128, D], BF16)) for i in range(2)]
            psw = [W(nc.psum_tensor(f"psw{i}", [128, 512], F32)) for i in range(4)]
            psb = [W(nc.psum_tensor(f"psb{i}", [128, 512], BF16)) for i in range(2)]

            def ld_xot(tt):
                sld(xot[tt % 3][:],
                    xtm.rearrange("(j p) d -> p j d", p=128)[:, tt, :],
                    f"xot{tt % 3}", f"ld_xo{tt}")

            ld_xot(0)
            ld_xot(1)
            for tt in range(NTILE):
                if tt + 2 < NTILE:
                    ld_xot(tt + 2)
                elif tt == NTILE - 2:
                    P.op("sp", lambda sx: sx.dma_start(
                        out=dummy[0:1, 32:64], in_=ident[0:1, 32:64]),
                        writes=["dummy_sp"], tag="sp_pad_wo")
                xbuf = xot[tt % 3]
                xbname = f"xot{tt % 3}"
                tbuf = t1[tt % 2]
                tbname = f"t1_{tt % 2}"
                for dh in range(2):
                    bank = psw[(2 * tt + dh) % 4]
                    bname = f"psw{(2 * tt + dh) % 4}"

                    def womm(t, bank=bank, tt=tt, dh=dh):
                        for hc in range(8):
                            ins = t.matmul(
                                bank[:],
                                aall[:, hc * T + tt * 128: hc * T + tt * 128 + 128],
                                wob[:, hc * D + dh * 512: hc * D + dh * 512 + 512],
                                start=(hc == 0), stop=(hc == 7))
                        return ins
                    P.op("pe", womm, reads=["aall", "wob"], writes=[bname],
                         tag=f"womm{tt}.{dh}")

                    def wadd(v, bank=bank, tbuf=tbuf, xbuf=xbuf, dh=dh):
                        return v.tensor_tensor(tbuf[:, dh * 512:(dh + 1) * 512],
                                               bank[:], xbuf[:, dh * 512:(dh + 1) * 512],
                                               ALU.add)
                    P.op("dve", wadd, reads=[bname, xbname], writes=[tbname],
                         tag=f"wadd{tt}.{dh}")

                sc_buf = t1[(tt + 1) % 2]
                sc_name = f"t1_{(tt + 1) % 2}"

                P.op("dve", lambda v, tbuf=tbuf: v.reduce_sum(st_s1, tbuf[:], axis=AX.X),
                     reads=[tbname], writes=["smallA"], force_sync=True, tag=f"ln1s1_{tt}")
                P.op("dve", lambda v: v.tensor_scalar(st_nm, st_s1, -1.0 / D, None,
                                                      ALU.mult),
                     reads=["smallA"], writes=["smallA"], force_sync=True,
                     tag=f"ln1nm_{tt}")
                P.op("dve", lambda v, tbuf=tbuf, sc_buf=sc_buf: v.tensor_scalar(
                    sc_buf[:], tbuf[:], st_nm, None, ALU.add),
                    reads=[tbname, "smallA"], writes=[sc_name], force_sync=True,
                    tag=f"ln1ctr_{tt}")
                P.op("dve", lambda v, sc_buf=sc_buf: v.tensor_tensor(
                    sc_buf[:], sc_buf[:], sc_buf[:], ALU.mult),
                    reads=[sc_name], writes=[sc_name], force_sync=True,
                    tag=f"ln1sq_{tt}")
                P.op("dve", lambda v, sc_buf=sc_buf: v.reduce_sum(
                    st_s2, sc_buf[:], axis=AX.X),
                    reads=[sc_name], writes=["smallA"], force_sync=True,
                    tag=f"ln1s2_{tt}")
                P.op("dve", lambda v: v.tensor_scalar(st_var, st_s2, 1.0 / D, None,
                                                      ALU.mult),
                     reads=["smallA"], writes=["smallA"], force_sync=True,
                     tag=f"ln1var_{tt}")
                P.op("act", lambda sc: sc.activation(st_l, st_var, AF.Ln, bias=st_eps),
                     reads=["smallA"], writes=["smallA"], force_sync=True,
                     tag=f"ln1l_{tt}")
                P.op("act", lambda sc: sc.activation(st_rstd, st_l, AF.Exp, scale=-0.5),
                     reads=["smallA"], writes=["smallA"], force_sync=True,
                     tag=f"ln1r_{tt}")
                P.op("dve", lambda v: v.tensor_scalar(st_rstd2, st_rstd, 2.0, None,
                                                      ALU.mult),
                     reads=["smallA"], writes=["smallA"], force_sync=True,
                     tag=f"ln1r2_{tt}")
                P.op("dve", lambda v, tbuf=tbuf, tt=tt: v.tensor_scalar(
                    out_acc[:, tt * D:(tt + 1) * D], tbuf[:], st_nm, st_rstd2,
                    ALU.add, ALU.mult),
                    reads=[tbname, "smallA"], writes=["out_acc"], force_sync=True,
                    tag=f"ln1acc_{tt}")
                x1tmb = x1tmb_[tt % 2]
                x1name = f"x1tmb{tt % 2}"
                P.op("dve", lambda v, tbuf=tbuf, x1tmb=x1tmb: v.tensor_scalar(
                    x1tmb[:], tbuf[:], st_nm, st_rstd, ALU.add, ALU.mult),
                    reads=[tbname, "smallA"], writes=[x1name], force_sync=True,
                    tag=f"ln1x1_{tt}")
                for half in range(2):
                    bank = psb[half]
                    bname = f"psb{half}"

                    def xtr(t, bank=bank, half=half, x1tmb=x1tmb):
                        for q in range(4):
                            ins = t.transpose(
                                bank[:, q * 128:(q + 1) * 128],
                                x1tmb[:, half * 512 + q * 128: half * 512 + (q + 1) * 128],
                                idnb[:])
                        return ins
                    P.op("pe", xtr, reads=[x1name, "idnb"], writes=[bname],
                         tag=f"xtr{tt}.{half}")

                    def xev(sc, bank=bank, tt=tt, half=half):
                        return sc.activation(
                            x1bf[:].rearrange("p (dc t) -> p dc t", dc=8)
                            [:, half * 4:(half + 1) * 4, tt * 128:(tt + 1) * 128],
                            bank[:].rearrange("p (q c) -> p q c", q=4),
                            AF.Copy)
                    P.op("act", xev, reads=[bname], writes=["x1bf"],
                         tag=f"xev{tt}.{half}")

        s1.close()
        P.fence()
        with ExitStack() as rctx:
            Rr = rctx.enter_context
            centb = Rr(nc.sbuf_tensor("centb", [128, 8 * 8], BF16))
            rbs = Rr(nc.sbuf_tensor("rbs", [128, 128], F32))
            rtmp = Rr(nc.sbuf_tensor("rtmp", [128, 6 * 128 + 64], F32))
            psr = Rr(nc.psum_tensor("psr", [128, 512], F32))

            bal = rtmp[:, 0:128]
            scr = rtmp[:, 128:256]
            rep = rtmp[:, 256:384]
            qtmp = rtmp[:, 384:512]
            msk1 = rtmp[:, 512:640]
            msk2 = rtmp[:, 640:768]
            g1 = rtmp[:, 768:784]
            g2 = rtmp[:, 784:800]
            w1 = rtmp[:, 800:816]
            w2 = rtmp[:, 816:832]

            gld(centb[:].rearrange("p (dc c) -> p dc c", dc=8),
                cent.rearrange("(dc p) c -> p dc c", p=128), "centb", "ld_cent")
            P.op("gp", lambda g: g.dma_start(out=dummy[0:1, 0:32], in_=ident[0:1, 0:32]),
                 writes=["dummy_tail"], tag="gp_pad_rt")
            sld(rbs[:], rbias, "rbs", "ld_rbias")
            P.op("sp", lambda sx: sx.dma_start(
                out=dummy[0:1, 32:64], in_=ident[0:1, 32:64]),
                writes=["dummy_sp"], tag="sp_pad_rt")

            for tt in range(NTILE):
                def rmm(t, tt=tt):
                    for dc in range(8):
                        ins = t.matmul(
                            psr[:, tt * 8:(tt + 1) * 8],
                            x1bf[:, dc * T + tt * 128: dc * T + tt * 128 + 128],
                            centb[:, dc * 8:(dc + 1) * 8],
                            start=(tt == 0 and dc == 0),
                            stop=(tt == NTILE - 1 and dc == 7))
                    return ins
                P.op("pe", rmm, reads=["x1bf", "centb"], writes=["psr"],
                     tag=f"rmm{tt}")
            P.op("act", lambda sc: sc.activation(scr, psr[:, 0:128], AF.Sigmoid),
                 reads=["psr"], writes=["rtmp"], tag="sig_scr")
            P.op("dve", lambda v: v.tensor_tensor(qtmp, psr[:, 0:128], rbs[:],
                                                  ALU.add),
                 reads=["psr", "rbs"], writes=["rtmp"], tag="rawb")
            P.op("act", lambda sc: sc.activation(bal, qtmp, AF.Sigmoid),
                 reads=["rtmp"], writes=["rtmp"], force_sync=True, tag="sig_bal")

            def seg(ap):
                return ap.rearrange("p (s e) -> p s e", e=8)

            def col3(ap):
                return ap.rearrange("p (s one) -> p s one", one=1)

            def dop(fn, tag):
                P.op("dve", fn, reads=["rtmp"], writes=["rtmp"], force_sync=True,
                     tag=tag)

            dop(lambda v: v.reduce_max(col3(g1), seg(bal), axis=AX.X), "r_m1")
            for e in range(8):
                dop(lambda v, e=e: v.tensor_copy(
                    seg(rep)[:, :, e:e + 1], col3(g1)), f"r_rep1{e}")
            dop(lambda v: v.tensor_tensor(msk1, bal, rep, ALU.is_ge), "r_msk1")
            dop(lambda v: v.scalar_tensor_tensor(qtmp, msk1, -1e30, bal,
                                                 ALU.mult, ALU.add), "r_balx")
            dop(lambda v: v.reduce_max(col3(g2), seg(qtmp), axis=AX.X), "r_m2")
            for e in range(8):
                dop(lambda v, e=e: v.tensor_copy(
                    seg(rep)[:, :, e:e + 1], col3(g2)), f"r_rep2{e}")
            dop(lambda v: v.tensor_tensor(msk2, qtmp, rep, ALU.is_ge), "r_msk2")
            dop(lambda v: v.tensor_tensor(qtmp, msk1, scr, ALU.mult), "r_q1")
            dop(lambda v: v.reduce_sum(col3(g1), seg(qtmp), axis=AX.X), "r_g1")
            dop(lambda v: v.tensor_tensor(qtmp, msk2, scr, ALU.mult), "r_q2")
            dop(lambda v: v.reduce_sum(col3(g2), seg(qtmp), axis=AX.X), "r_g2")
            dop(lambda v: v.tensor_tensor(w1, g1, g2, ALU.subtract), "r_d1")
            P.op("act", lambda sc: sc.activation(w2, w1, AF.Sigmoid, scale=-1.0),
                 reads=["rtmp"], writes=["rtmp"], force_sync=True, tag="r_w2")
            P.op("act", lambda sc: sc.activation(w1, w1, AF.Sigmoid),
                 reads=["rtmp"], writes=["rtmp"], force_sync=True, tag="r_w1")
            for e in range(8):
                dop(lambda v, e=e: v.tensor_copy(
                    seg(rep)[:, :, e:e + 1], col3(w1)), f"r_repw1{e}")
            dop(lambda v: v.tensor_tensor(qtmp, msk1, rep, ALU.mult), "r_c1")
            for e in range(8):
                dop(lambda v, e=e: v.tensor_copy(
                    seg(rep)[:, :, e:e + 1], col3(w2)), f"r_repw2{e}")
            dop(lambda v: v.tensor_tensor(rep, msk2, rep, ALU.mult), "r_c2")
            P.op("dve", lambda v: v.tensor_tensor(combine[:], qtmp, rep, ALU.add),
                 reads=["rtmp"], writes=["combine"], force_sync=True, tag="r_comb")

        P.fence()
        with ExitStack() as fctx:
            Ff = fctx.enter_context
            wfcb = [Ff(nc.sbuf_tensor(f"wfcb{i}", [128, 8 * 1024], BF16))
                    for i in range(2)]
            wprb = [Ff(nc.sbuf_tensor(f"wprb{i}", [128, 8 * 1024], BF16))
                    for i in range(2)]
            hb = Ff(nc.sbuf_tensor("hb", [128, 8 * T], BF16))
            psf = [Ff(nc.psum_tensor(f"psf{i}", [128, 512], F32)) for i in range(2)]
            psp = [Ff(nc.psum_tensor(f"psp{i}", [128, 512], F32)) for i in range(3)]

            NUNIT = 4
            units = [(e, u) for e in range(NE_R + NE_S) for u in range(NUNIT)]

            def load_unit(idx):
                e, u = units[idx]
                slot = idx % 2
                gld(wfcb[slot][:].rearrange("p (dc f) -> p dc f", dc=8),
                    wfc[e].rearrange("(dc p) f -> p dc f", p=128)
                    [:, :, u * 1024:(u + 1) * 1024],
                    f"wfcb{slot}", f"ldfc{idx}")
                gld(wprb[slot][:].rearrange("p (fc d) -> p fc d", fc=8),
                    wpr[e].rearrange("(fc p) d -> p fc d", p=128)
                    [:, u * 8:(u + 1) * 8, :],
                    f"wprb{slot}", f"ldpr{idx}")

            load_unit(0)
            load_unit(1)
            fcg = [0]
            prg = [0]
            for idx, (e, u) in enumerate(units):
                slot = idx % 2
                wf, wp = wfcb[slot], wprb[slot]
                wfn, wpn = f"wfcb{slot}", f"wprb{slot}"
                for Ft in range(8):
                    for tc in range(4):
                        bank = psf[fcg[0] % 2]
                        bname = f"psf{fcg[0] % 2}"
                        fcg[0] += 1

                        def fcmm(t, bank=bank, wf=wf, Ft=Ft, tc=tc):
                            for dc in range(8):
                                ins = t.matmul(
                                    bank[:],
                                    wf[:, dc * 1024 + Ft * 128: dc * 1024 + (Ft + 1) * 128],
                                    x1bf[:, dc * T + tc * 512: dc * T + tc * 512 + 512],
                                    start=(dc == 0), stop=(dc == 7))
                            return ins
                        P.op("pe", fcmm, reads=[wfn, "x1bf"], writes=[bname],
                             tag=f"fc{idx}.{Ft}.{tc}")

                        def gel(sc, bank=bank, Ft=Ft, tc=tc):
                            return sc.activation(
                                hb[:, Ft * T + tc * 512: Ft * T + tc * 512 + 512],
                                bank[:], AF.Gelu)
                        P.op("act", gel, reads=[bname], writes=["hb"],
                             tag=f"gel{idx}.{Ft}.{tc}")
                for tt in range(NTILE):
                    for dh in range(2):
                        bank = psp[prg[0] % 3]
                        bname = f"psp{prg[0] % 3}"
                        prg[0] += 1

                        def prmm(t, bank=bank, wp=wp, tt=tt, dh=dh):
                            for fc_ in range(8):
                                ins = t.matmul(
                                    bank[:],
                                    hb[:, fc_ * T + tt * 128: fc_ * T + tt * 128 + 128],
                                    wp[:, fc_ * 1024 + dh * 512: fc_ * 1024 + dh * 512 + 512],
                                    start=(fc_ == 0), stop=(fc_ == 7))
                            return ins
                        P.op("pe", prmm, reads=["hb", wpn], writes=[bname],
                             tag=f"pr{idx}.{tt}.{dh}")

                        if e < NE_R:
                            def ev(v, bank=bank, tt=tt, dh=dh, e=e):
                                dst = out_acc[:, tt * D + dh * 512:
                                              tt * D + dh * 512 + 512]
                                return v.scalar_tensor_tensor(
                                    dst, bank[:], combine[:, tt * 8 + e:tt * 8 + e + 1],
                                    dst, ALU.mult, ALU.add)
                            P.op("dve", ev, reads=[bname, "combine", "out_acc"],
                                 writes=["out_acc"], tag=f"ev{idx}.{tt}.{dh}")
                        else:
                            def ev(v, bank=bank, tt=tt, dh=dh):
                                dst = out_acc[:, tt * D + dh * 512:
                                              tt * D + dh * 512 + 512]
                                return v.tensor_tensor(dst, bank[:], dst, ALU.add)
                            P.op("dve", ev, reads=[bname, "out_acc"],
                                 writes=["out_acc"], tag=f"ev{idx}.{tt}.{dh}")
                if idx + 2 < len(units):
                    load_unit(idx + 2)
                    if idx + 2 == len(units) - 1:
                        P.op("gp", lambda g: g.dma_start(
                            out=dummy[0:1, 0:32], in_=ident[0:1, 0:32]),
                            writes=["dummy_tail"], tag="gp_pad_ffn")

        P.fence()
        with ExitStack() as lctx:
            L = lctx.enter_context
            ysb = [L(nc.sbuf_tensor(f"ysb{i}", [128, D], F32)) for i in range(2)]
            sqb = L(nc.sbuf_tensor("sqb", [128, D], F32))

            for tt in range(NTILE):
                src = out_acc[:, tt * D:(tt + 1) * D]
                P.op("dve", lambda v, src=src: v.reduce_sum(st_s1, src, axis=AX.X),
                     reads=["out_acc"], writes=["smallA"], force_sync=True,
                     tag=f"ln2s1_{tt}")
                P.op("dve", lambda v: v.tensor_scalar(st_nm, st_s1, -1.0 / D, None,
                                                      ALU.mult),
                     reads=["smallA"], writes=["smallA"], force_sync=True,
                     tag=f"ln2nm_{tt}")
                P.op("dve", lambda v, src=src: v.tensor_scalar(
                    sqb[:], src, st_nm, None, ALU.add),
                    reads=["out_acc", "smallA"], writes=["sqb"], force_sync=True,
                    tag=f"ln2ctr_{tt}")
                P.op("dve", lambda v: v.tensor_tensor(sqb[:], sqb[:], sqb[:],
                                                      ALU.mult),
                     reads=["sqb"], writes=["sqb"], force_sync=True,
                     tag=f"ln2sq_{tt}")
                P.op("dve", lambda v: v.reduce_sum(st_s2, sqb[:], axis=AX.X),
                     reads=["sqb"], writes=["smallA"], force_sync=True,
                     tag=f"ln2s2_{tt}")
                P.op("dve", lambda v: v.tensor_scalar(st_var, st_s2, 1.0 / D, None,
                                                      ALU.mult),
                     reads=["smallA"], writes=["smallA"], force_sync=True,
                     tag=f"ln2var_{tt}")
                P.op("act", lambda sc: sc.activation(st_l, st_var, AF.Ln, bias=st_eps),
                     reads=["smallA"], writes=["smallA"], force_sync=True,
                     tag=f"ln2l_{tt}")
                P.op("act", lambda sc: sc.activation(st_rstd, st_l, AF.Exp, scale=-0.5),
                     reads=["smallA"], writes=["smallA"], force_sync=True,
                     tag=f"ln2r_{tt}")
                ybuf = ysb[tt % 2]
                ybname = f"ysb{tt % 2}"
                P.op("dve", lambda v, src=src, ybuf=ybuf: v.tensor_scalar(
                    ybuf[:], src, st_nm, st_rstd, ALU.add, ALU.mult),
                    reads=["out_acc", "smallA"], writes=[ybname], force_sync=True,
                    tag=f"ln2y_{tt}")
                P.op("sp", lambda s, ybuf=ybuf, tt=tt: s.dma_start(
                    out=y.rearrange("(j p) d -> p j d", p=128)[:, tt, :],
                    in_=ybuf[:]),
                    reads=[ybname], writes=["y"], tag=f"st{tt}")

        sp_final = P.counts["sp"]

        @block.tensor
        def _(t):
            P.emit_engine("pe", t)

        @block.scalar
        def _(sc):
            P.emit_engine("act", sc)

        @block.vector
        def _(v):
            P.emit_engine("dve", v)

        @block.sync
        def _(s):
            P.emit_engine("sp", s)
            s.wait_ge(P.sems["sp"], sp_final)

        @block.gpsimd
        def _(g):
            P.emit_engine("gp", g)

    return nc


_NC_CACHE = None


def kernel(L0_attn_x, L0_attn_WQ, L0_attn_WK, L0_attn_WV, L0_attn_WO,
           L0_ffn_centroids, L0_ffn_routing_bias, L0_ffn_Wfc_r, L0_ffn_Wproj_r,
           L0_ffn_Wfc_sh, L0_ffn_Wproj_sh):
    global _NC_CACHE
    x = np.asarray(L0_attn_x, np.float32)

    wq = np.ascontiguousarray(np.asarray(L0_attn_WQ, np.float32)
                              .transpose(2, 0, 1).reshape(D, D))
    wk = np.ascontiguousarray(np.asarray(L0_attn_WK, np.float32)
                              .transpose(2, 0, 1).reshape(D, D))
    wv = np.ascontiguousarray(np.asarray(L0_attn_WV, np.float32)
                              .transpose(2, 0, 1).reshape(D, D))
    wo = np.ascontiguousarray(np.asarray(L0_attn_WO, np.float32)
                              .transpose(0, 2, 1).reshape(D, D))
    cent = np.ascontiguousarray(np.asarray(L0_ffn_centroids, np.float32).T)
    rbias = np.ascontiguousarray(np.broadcast_to(
        np.tile(np.asarray(L0_ffn_routing_bias, np.float32), NTILE)[None, :],
        (128, 128)))
    wfc = np.ascontiguousarray(np.concatenate(
        [np.asarray(L0_ffn_Wfc_r, np.float32),
         np.asarray(L0_ffn_Wfc_sh, np.float32)], axis=0).transpose(0, 2, 1))
    wpr = np.ascontiguousarray(np.concatenate(
        [np.asarray(L0_ffn_Wproj_r, np.float32),
         np.asarray(L0_ffn_Wproj_sh, np.float32)], axis=0).transpose(0, 2, 1))
    ident = np.eye(128, dtype=np.float32)
    identb = np.eye(128, dtype=ml_dtypes.bfloat16)

    xts = [np.ascontiguousarray(x[b].T) for b in range(2)]

    in_maps = []
    for c in range(8):
        b, k = c // 4, c % 4
        rows = np.concatenate(
            [np.arange(128) + 512 * j + 128 * k for j in range(NTILE)])
        xtm_c = np.ascontiguousarray(x[b][rows, :])
        xt_own = np.ascontiguousarray(xts[b][:, rows])
        mask = np.zeros((4, 128, 128), np.float32)
        for bb in range(4):
            kk = np.arange(128)[:, None]
            rr = np.arange(128)[None, :]
            mask[bb] = np.where(128 * bb + kk <= 128 * k + rr, 0.0, -1e30)
        in_maps.append(dict(
            xt=xts[b], xtm=xtm_c, xt_own=xt_own,
            wq=wq, wk=wk, wv=wv, wo=wo, cent=cent, rbias=rbias,
            wfc=wfc, wpr=wpr,
            mask01=mask.astype(ml_dtypes.bfloat16), ident=ident, identb=identb,
        ))

    if _NC_CACHE is None:
        _NC_CACHE = build_program()
    res = run_bass_kernel_spmd(_NC_CACHE, in_maps, core_ids=list(range(8)))

    out = np.empty((2, S, D), np.float32)
    for c in range(8):
        b, k = c // 4, c % 4
        yc = np.asarray(res.results[c]["y"], np.float32)
        for j in range(NTILE):
            out[b, 512 * j + 128 * k: 512 * j + 128 * k + 128, :] = \
                yc[j * 128:(j + 1) * 128, :]
    return out


# version 13
# speedup vs baseline: 1.7885x; 1.7057x over previous
import numpy as np
import ml_dtypes
from collections import defaultdict
from contextlib import ExitStack

import concourse.bass as bass
import concourse.mybir as mybir
from concourse.bass_utils import run_bass_kernel_spmd

F32 = mybir.dt.float32
BF16 = mybir.dt.bfloat16
AF = mybir.ActivationFunctionType
ALU = mybir.AluOpType
AX = mybir.AxisListType

S = 8192
T = 2048
D = 1024
NHEADS = 16
KC = 64
NTILE = 16
NE_R = 8
NE_S = 2
F_FFN = 4096
NEG = -3.0e38
EPS = 1e-5


class _Res:
    __slots__ = ("writer", "readers")

    def __init__(self):
        self.writer = None
        self.readers = []


class Plan:
    ENGINES = {"pe": 1, "act": 1, "dve": 1, "sp": 16, "gp": 16}

    def __init__(self, nc, ctx):
        self.nc = nc
        self.sems = {}
        self.counts = {}
        self.ops = {e: [] for e in self.ENGINES}
        self.satisfied = {e: defaultdict(int) for e in self.ENGINES}
        for e in self.ENGINES:
            self.sems[e] = ctx.enter_context(nc.semaphore(f"sem_{e}"))
            self.counts[e] = 0
        self._res = {}
        self._fence = None
        self._fence_applied = set()
        self._gseq = 0
        self._dma_seq = {"gp": {}, "sp": {}}

    def res(self, name):
        r = self._res.get(name)
        if r is None:
            r = _Res()
            self._res[name] = r
        return r

    def fence(self):
        self._fence = [(e, self.counts[e]) for e in self.ENGINES if self.counts[e]]
        self._fence_applied = set()

    def op(self, eng, emit, reads=(), writes=(), extra_waits=(), force_sync=False,
           tag=""):
        reads = [self.res(r) if isinstance(r, str) else r for r in reads]
        writes = [self.res(r) if isinstance(r, str) else r for r in writes]
        want = {}

        def need(tok):
            if tok is None:
                return
            s, c = tok
            if want.get(s, 0) < c:
                want[s] = c

        for r in reads:
            need(r.writer)
        for r in writes:
            need(r.writer)
            for t in r.readers:
                need(t)
        for t in extra_waits:
            need(t)
        if self._fence is not None and eng not in self._fence_applied:
            for t in self._fence:
                need(t)
            self._fence_applied.add(eng)
            if len(self._fence_applied) == len(self.ENGINES):
                self._fence = None

        waits = []
        sat = self.satisfied[eng]
        for s, c in sorted(want.items()):
            if s == eng and not force_sync and c <= self.counts[eng]:
                continue
            if s == eng and force_sync and c <= self.counts[eng]:
                if sat[s] >= c:
                    continue
                waits.append((s, c))
                sat[s] = c
                continue
            if sat[s] >= c:
                continue
            waits.append((s, c))
            sat[s] = c

        step = self.ENGINES[eng]
        self.counts[eng] += step
        tok = (eng, self.counts[eng])
        seq = self._gseq
        self._gseq += 1
        if step == 16:
            self._dma_seq[eng][self.counts[eng]] = seq
        self.ops[eng].append((waits, emit, step, tag, seq))
        for r in reads:
            r.readers.append(tok)
        for r in writes:
            r.writer = tok
            r.readers = []
        return tok

    def emit_engine(self, eng, obj):
        sems = self.sems
        for waits, emit, step, tag, seq in self.ops[eng]:
            for s, c in waits:
                if s in ("gp", "sp"):
                    nseq = self._dma_seq[s].get(c + 16)
                    if nseq is not None and nseq < seq:
                        c = c + 16
                obj.wait_ge(sems[s], c)
            last = emit(obj)
            assert last is not None, f"emit {tag!r} on {eng} returned None"
            last.then_inc(sems[eng], step)


def build_program():
    nc = bass.Bass("TRN2", target_bir_lowering=False, debug=False)

    xt = nc.dram_tensor("xt", [D, S], F32, kind="ExternalInput").ap()
    xto = nc.dram_tensor("xt_own", [D, T], F32, kind="ExternalInput").ap()
    xtm = nc.dram_tensor("xtm", [T, D], F32, kind="ExternalInput").ap()
    wqd = nc.dram_tensor("wq", [D, D], F32, kind="ExternalInput").ap()
    wkd = nc.dram_tensor("wk", [D, D], F32, kind="ExternalInput").ap()
    wvd = nc.dram_tensor("wv", [D, D], F32, kind="ExternalInput").ap()
    wod = nc.dram_tensor("wo", [D, D], F32, kind="ExternalInput").ap()
    cent = nc.dram_tensor("cent", [D, 8], F32, kind="ExternalInput").ap()
    rbias = nc.dram_tensor("rbias", [128, 128], F32, kind="ExternalInput").ap()
    wfc = nc.dram_tensor("wfc", [NE_R + NE_S, D, F_FFN], F32, kind="ExternalInput").ap()
    wpr = nc.dram_tensor("wpr", [NE_R + NE_S, F_FFN, D], F32, kind="ExternalInput").ap()
    mask01 = nc.dram_tensor("mask01", [4, 128, 128], BF16, kind="ExternalInput").ap()
    ident = nc.dram_tensor("ident", [128, 128], F32, kind="ExternalInput").ap()
    identb = nc.dram_tensor("identb", [128, 128], BF16, kind="ExternalInput").ap()
    y = nc.dram_tensor("y", [T, D], F32, kind="ExternalOutput").ap()
    x16 = nc.dram_tensor("x16s", [D, S], BF16, kind="Internal").ap()
    x16q = nc.dram_tensor("x16q", [D, T], BF16, kind="Internal").ap()

    with ExitStack() as ctx:
        E = ctx.enter_context
        P = Plan(nc, ctx)
        block = E(nc.Block())

        m01 = E(nc.sbuf_tensor("m01", [128, 4 * 128], BF16))
        idn = E(nc.sbuf_tensor("idn", [128, 128], F32))
        idnb = E(nc.sbuf_tensor("idnb", [128, 128], BF16))
        ones1 = E(nc.sbuf_tensor("ones1", [1, 128], F32))
        smallA = E(nc.sbuf_tensor("smallA", [128, 64], F32))
        dummy = E(nc.sbuf_tensor("dummy_tail", [1, 64], F32))
        s1 = ExitStack()
        aall = s1.enter_context(nc.sbuf_tensor("aall", [128, 8 * T], BF16, side="right"))

        strip = smallA[:, 0:16]
        negm = smallA[:, 16:32]
        st_s1 = smallA[:, 32:33]
        st_nm = smallA[:, 33:34]
        st_s2 = smallA[:, 34:35]
        st_var = smallA[:, 35:36]
        st_l = smallA[:, 36:37]
        st_rstd = smallA[:, 37:38]
        st_rstd2 = smallA[:, 38:39]
        st_eps = smallA[:, 39:40]

        def gld(dst, src, name, tag):
            P.op("gp", lambda g, dst=dst, src=src: g.dma_start(out=dst, in_=src),
                 writes=[name], tag=tag)

        def sld(dst, src, name, tag):
            P.op("sp", lambda s, dst=dst, src=src: s.dma_start(out=dst, in_=src),
                 writes=[name], tag=tag)

        sld(m01[:].rearrange("p (b c) -> p b c", b=4),
            mask01.rearrange("b k c -> k b c"), "m01", "ld_m01")
        sld(idn[:], ident, "idn", "ld_idn")
        sld(idnb[:], identb, "idnb", "ld_idnb")
        P.op("dve", lambda v: v.memset(ones1[:], 1.0), writes=["ones1"], tag="ones")

        with ExitStack() as actx:
            A = actx.enter_context
            qa = [A(nc.sbuf_tensor(f"qa{i}", [65, T], BF16)) for i in range(4)]
            rsig = A(nc.sbuf_tensor("rsig", [1, 512], F32))
            lsig = A(nc.sbuf_tensor("lsig", [1, 512], F32))
            bcs = A(nc.sbuf_tensor("bcs", [64, 512], F32))

            ps = [A(nc.psum_tensor(f"ps{i}", [128, 512], F32)) for i in range(8)]

            wqb = A(nc.sbuf_tensor("wqb", [128, 8 * D], BF16))
            wkb = A(nc.sbuf_tensor("wkb", [128, 8 * D], BF16))
            wvb = A(nc.sbuf_tensor("wvb", [128, 8 * D], BF16))
            ka = [A(nc.sbuf_tensor(f"ka{i}", [65, S], BF16)) for i in range(3)]
            va = [A(nc.sbuf_tensor(f"va{i}", [128, KC * 65], BF16)) for i in range(3)]
            xc = [A(nc.sbuf_tensor(f"xc{i}", [128, 8 * 256], BF16)) for i in range(3)]
            eb = [A(nc.sbuf_tensor(f"eb{i}", [128, T], BF16)) for i in range(4)]
            gld(wqb[:].rearrange("p (dc c) -> p dc c", dc=8),
                wqd.rearrange("(dc p) c -> p dc c", p=128), "wqb", "ld_wq")
            gld(wkb[:].rearrange("p (dc c) -> p dc c", dc=8),
                wkd.rearrange("(dc p) c -> p dc c", p=128), "wkb", "ld_wk")
            gld(wvb[:].rearrange("p (dc c) -> p dc c", dc=8),
                wvd.rearrange("(dc p) c -> p dc c", p=128), "wvb", "ld_wv")

            def init_attn(v):
                for i in range(3):
                    v.memset(va[i][:].rearrange("p (kc c) -> p kc c", c=65)[:, :, 64:65], 1.0)
                    v.memset(ka[i][64:65, :], 1.0)
                for i in range(4):
                    v.memset(qa[i][64:65, :], 0.0)
                v.memset(smallA[:], 0.0)
                return v.memset(st_eps, EPS)
            P.op("dve", init_attn, writes=["init_scratch"], tag="init_attn_a")
            P.op("dve", init_attn,
                 writes=["smallA"] + [f"va{i}" for i in range(3)] + [f"ka{i}" for i in range(3)],
                 tag="init_attn")

            rot01 = [0]

            def gen_kvq(beta):
                pq = beta
                sk = [(2 * beta) % 3, (2 * beta + 1) % 3]
                sq = [(2 * beta) % 4, (2 * beta + 1) % 4]

                def emit_load(i):
                    if i >= 40:
                        P.op("gp", lambda g: g.dma_start(
                            out=dummy[0:1, 0:32], in_=ident[0:1, 0:32]),
                            writes=["dummy_tail"], tag=f"gp_pad{beta}")
                        P.op("sp", lambda s: s.dma_start(
                            out=dummy[0:1, 32:64], in_=ident[0:1, 32:64]),
                            writes=["dummy_sp"], tag=f"sp_pad{beta}")
                        return
                    buf, bufname = xc[i % 3], f"xc{i % 3}"
                    if i < 32:
                        src32 = xt.rearrange("(dc p) t -> p dc t", p=128)[
                            :, :, i * 256:(i + 1) * 256]
                        s16 = x16.rearrange("(dc p) t -> p dc t", p=128)[
                            :, :, i * 256:(i + 1) * 256]
                        nm = f"x16.{i}"
                    else:
                        qc = i - 32
                        src32 = xto.rearrange("(dc p) t -> p dc t", p=128)[
                            :, :, qc * 256:(qc + 1) * 256]
                        s16 = x16q.rearrange("(dc p) t -> p dc t", p=128)[
                            :, :, qc * 256:(qc + 1) * 256]
                        nm = f"x16q.{i}"
                    bview = buf[:].rearrange("p (dc c) -> p dc c", dc=8)
                    if beta == 0:
                        gld(bview, src32, bufname, f"ld_x{beta}.{i}")
                        P.op("sp", lambda sx, s16=s16, bview=bview: sx.dma_start(
                            out=s16, in_=bview),
                            reads=[bufname], writes=[nm], tag=f"st_x16.{i}")
                    else:
                        P.op("sp", lambda sx, s16=s16, bview=bview: sx.dma_start(
                            out=bview, in_=s16),
                            reads=[nm], writes=[bufname], tag=f"ld16_{beta}.{i}")

                first = [True]
                for ck in range(32):
                    def unit(ck=ck):
                        if first[0]:
                            emit_load(0)
                            emit_load(1)
                            first[0] = False
                        emit_load(ck + 2)
                        buf = xc[ck % 3]
                        bufname = f"xc{ck % 3}"
                        bank = ps[rot01[0] % 2]
                        bname = f"ps{rot01[0] % 2}"
                        rot01[0] += 1

                        def pe_k(t, bank=bank, buf=buf):
                            for dc in range(8):
                                ins = t.matmul(
                                    bank[:, 0:256],
                                    wkb[:, dc * D + pq * 128: dc * D + pq * 128 + 128],
                                    buf[:, dc * 256:(dc + 1) * 256],
                                    start=(dc == 0), stop=(dc == 7))
                            return ins
                        P.op("pe", pe_k, reads=["wkb", bufname], writes=[bname],
                             tag=f"Kmm{beta}.{ck}")

                        def a_k(v, bank=bank, ck=ck):
                            v.tensor_copy(ka[sk[0]][0:64, ck * 256:(ck + 1) * 256],
                                          bank[0:64, 0:256])
                            return v.tensor_copy(
                                ka[sk[1]][0:64, ck * 256:(ck + 1) * 256],
                                bank[64:128, 0:256])
                        P.op("dve", a_k, reads=[bname],
                             writes=[f"ka{sk[0]}", f"ka{sk[1]}"], tag=f"Kev{beta}.{ck}")

                        bank = ps[rot01[0] % 2]
                        bname = f"ps{rot01[0] % 2}"
                        rot01[0] += 1

                        def pe_v(t, bank=bank, buf=buf):
                            ins = None
                            for sub in range(2):
                                for dc in range(8):
                                    ins = t.matmul(
                                        bank[:, sub * 128:(sub + 1) * 128],
                                        buf[:, dc * 256 + sub * 128: dc * 256 + sub * 128 + 128],
                                        wvb[:, dc * D + pq * 128: dc * D + pq * 128 + 128],
                                        start=(sub == 0 and dc == 0),
                                        stop=(sub == 1 and dc == 7))
                            return ins
                        P.op("pe", pe_v, reads=["wvb", bufname], writes=[bname],
                             tag=f"Vmm{beta}.{ck}")

                        def a_v(v, bank=bank, ck=ck):
                            ins = None
                            for sub in range(2):
                                kc = ck * 2 + sub
                                for i in range(2):
                                    ins = v.tensor_copy(
                                        va[sk[i]][:, kc * 65: kc * 65 + 64],
                                        bank[:, sub * 128 + i * 64: sub * 128 + i * 64 + 64])
                            return ins
                        P.op("dve", a_v, reads=[bname],
                             writes=[f"va{sk[0]}", f"va{sk[1]}"], tag=f"Vev{beta}.{ck}")
                    yield unit
                for qc in range(8):
                    def unit(qc=qc):
                        emit_load(32 + qc + 2)
                        buf = xc[(32 + qc) % 3]
                        bufname = f"xc{(32 + qc) % 3}"
                        bank = ps[rot01[0] % 2]
                        bname = f"ps{rot01[0] % 2}"
                        rot01[0] += 1

                        def pe_q(t, bank=bank, buf=buf):
                            for dc in range(8):
                                ins = t.matmul(
                                    bank[:, 0:256],
                                    wqb[:, dc * D + pq * 128: dc * D + pq * 128 + 128],
                                    buf[:, dc * 256:(dc + 1) * 256],
                                    start=(dc == 0), stop=(dc == 7))
                            return ins
                        P.op("pe", pe_q, reads=["wqb", bufname], writes=[bname],
                             tag=f"Qmm{beta}.{qc}")

                        def a_q(v, bank=bank, qc=qc):
                            v.tensor_scalar(qa[sq[0]][0:64, qc * 256:(qc + 1) * 256],
                                            bank[0:64, 0:256], 0.125, None, ALU.mult)
                            return v.tensor_scalar(
                                qa[sq[1]][0:64, qc * 256:(qc + 1) * 256],
                                bank[64:128, 0:256], 0.125, None, ALU.mult)
                        P.op("dve", a_q, reads=[bname],
                             writes=[f"qa{sq[0]}", f"qa{sq[1]}"], tag=f"Qev{beta}.{qc}")
                    yield unit

            mcol = smallA[:, 0:1]
            mval = smallA[0:1, 2:3]

            def gen_stab(h):
                ks = h % 3
                qs = h % 4
                kah, qah = f"ka{ks}", f"qa{qs}"

                def unit_a(h=h, ks=ks, qs=qs):
                    def p1mm(t):
                        return t.matmul(ps[0][:], qa[qs][0:64, 15 * 128:16 * 128],
                                        ka[ks][0:64, 0:512], start=True, stop=True)
                    P.op("pe", p1mm, reads=[kah, qah], writes=["ps0"],
                         tag=f"p1mm{h}")

                    def p1red(v):
                        return v.reduce_max(mcol, ps[0][:, 0:509:4], axis=AX.X)
                    P.op("dve", p1red, reads=["ps0"], writes=["smallA"],
                         tag=f"p1red{h}")
                yield unit_a

                def unit_b(h=h, qs=qs, qah=qah):
                    def mtr(t):
                        return t.transpose(ps[0][0:1, 0:128], mcol, idn[:])
                    P.op("pe", mtr, reads=["smallA", "idn"], writes=["ps0"],
                         tag=f"mtr{h}")

                    def mred(v):
                        return v.reduce_max(mval, ps[0][0:1, 0:128], axis=AX.X,
                                            negate=True)
                    P.op("dve", mred, reads=["ps0"], writes=["smallA"],
                         force_sync=True, tag=f"mred{h}")

                    def mbc(v, qs=qs):
                        return v.tensor_scalar(qa[qs][64:65, :], qa[qs][64:65, :],
                                               0.0, mval, ALU.mult, ALU.add)
                    P.op("dve", mbc, reads=["smallA", qah], writes=[qah],
                         force_sync=True, tag=f"mbc{h}")
                yield unit_b

            def pieces_for(kc):
                c0 = (kc // 4) * 128
                return [(p, max(c0, p * 512), min(T, (p + 1) * 512))
                        for p in range(4) if (p + 1) * 512 > c0]

            def gen_pass2(h):
                ks = h % 3
                qs = h % 4
                kah, qah, vah = f"ka{ks}", f"qa{qs}", f"va{ks}"
                rot = [0]
                for kc in range(KC):
                    ebuf = eb[kc % 4]
                    ebname = f"eb{kc % 4}"
                    c0 = (kc // 4) * 128
                    pcs = pieces_for(kc)

                    def unit_a(h=h, kc=kc, ebuf=ebuf, ebname=ebname, c0=c0,
                               pcs=pcs, ks=ks, qs=qs, kah=kah, qah=qah, rot=rot):
                        for (p, a, b) in pcs:
                            bank = ps[rot[0] % 4]
                            bname = f"ps{rot[0] % 4}"
                            rot[0] += 1

                            def p2mm(t, bank=bank, kc=kc, a=a, b=b):
                                return t.matmul(bank[:, 0:b - a],
                                                ka[ks][0:65, kc * 128:(kc + 1) * 128],
                                                qa[qs][0:65, a:b], start=True, stop=True)
                            P.op("pe", p2mm, reads=[kah, qah], writes=[bname],
                                 tag=f"p2mm{h}.{kc}.{p}")

                            if a == c0:
                                def mska(v, bank=bank, kc=kc, a=a, c0=c0):
                                    return v.tensor_tensor(
                                        bank[:, c0 - a:c0 - a + 128],
                                        bank[:, c0 - a:c0 - a + 128],
                                        m01[:, (kc % 4) * 128:(kc % 4 + 1) * 128],
                                        ALU.add)
                                P.op("dve", mska, reads=[bname, "m01"],
                                     writes=[bname], tag=f"mska{h}.{kc}")

                            def aexp(sc, bank=bank, a=a, b=b):
                                return sc.activation(ebuf[:, a:b], bank[:, 0:b - a],
                                                     AF.Exp)
                            P.op("act", aexp, reads=[bname], writes=[ebname],
                                 tag=f"exp{h}.{kc}.{p}")
                    yield unit_a

                    def unit_b(h=h, kc=kc, ebuf=ebuf, ebname=ebname, pcs=pcs,
                               ks=ks, vah=vah):
                        for (p, a, b) in pcs:
                            pvname = f"ps{4 + p}"

                            def pv(t, kc=kc, p=p, a=a, b=b):
                                return t.matmul(
                                    ps[4 + p][0:65, a - p * 512:b - p * 512],
                                    va[ks][:, kc * 65:(kc + 1) * 65],
                                    ebuf[:, a:b],
                                    start=(kc == 0), stop=(kc == 16 * p + 15))
                            P.op("pe", pv, reads=[vah, ebname], writes=[pvname],
                                 tag=f"pv{h}.{kc}.{p}")
                    yield unit_b

            def gen_norm(h):
                for p in range(4):
                    def unit_rcp(h=h, p=p):
                        def rln(sc, p=p):
                            return sc.activation(lsig[0:1, 0:512],
                                                 ps[4 + p][64:65, 0:512], AF.Ln)
                        P.op("act", rln, reads=[f"ps{4 + p}"], writes=["lsig"],
                             tag=f"rln{h}.{p}")

                        def rex(sc):
                            return sc.activation(rsig[0:1, 0:512], lsig[0:1, 0:512],
                                                 AF.Exp, scale=-1.0)
                        P.op("act", rex, reads=["lsig"], writes=["rsig"],
                             force_sync=True, tag=f"rex{h}.{p}")
                    yield unit_rcp

                    def unit_nrm(h=h, p=p):
                        pvname = f"ps{4 + p}"

                        def bcmm(t, p=p):
                            return t.matmul(ps[2][0:64, 0:512], ones1[0:1, 0:64],
                                            rsig[0:1, 0:512], start=True, stop=True)
                        P.op("pe", bcmm, reads=["ones1", "rsig"], writes=["ps2"],
                             tag=f"bcmm{h}.{p}")

                        def bcev(v):
                            return v.tensor_copy(bcs[:], ps[2][0:64, 0:512])
                        P.op("dve", bcev, reads=["ps2"], writes=["bcs"],
                             tag=f"bcev{h}.{p}")

                        def nrm(v, h=h, p=p):
                            return v.tensor_tensor(
                                aall[(h % 2) * 64:(h % 2) * 64 + 64,
                                     (h // 2) * T + p * 512:(h // 2) * T + p * 512 + 512],
                                ps[4 + p][0:64, 0:512], bcs[:], ALU.mult)
                        P.op("dve", nrm, reads=[pvname, "bcs"], writes=["aall"],
                             tag=f"nrm{h}.{p}")
                    yield unit_nrm

            def drain(gen):
                for u in gen:
                    u()

            def chain(*gens):
                for g in gens:
                    for u in g:
                        yield u

            def skew2(gen):
                it = iter(gen)
                try:
                    a0 = next(it)
                except StopIteration:
                    return
                pending_b = None
                yield a0
                while True:
                    try:
                        a_or_b = next(it)
                    except StopIteration:
                        break
                    if pending_b is None:
                        pending_b = a_or_b
                        try:
                            nxt_a = next(it)
                        except StopIteration:
                            yield pending_b
                            return
                        yield nxt_a
                        yield pending_b
                        pending_b = None
                    else:
                        raise AssertionError
                return

            def merge(g1, g2, ratio=2):
                done1 = done2 = False
                it1, it2 = iter(g1), iter(g2)
                while not (done1 and done2):
                    if not done1:
                        try:
                            next(it1)()
                        except StopIteration:
                            done1 = True
                    if not done2:
                        for _ in range(ratio):
                            try:
                                next(it2)()
                            except StopIteration:
                                done2 = True
                                break

            drain(gen_kvq(0))
            drain(gen_stab(0))
            for h in range(NHEADS):
                g2 = skew2(gen_pass2(h)) if h == 0 else chain(gen_norm(h - 1),
                                                              skew2(gen_pass2(h)))
                if h % 2 == 0:
                    merge(g2, gen_stab(h + 1), ratio=1)
                else:
                    beta = h // 2 + 1
                    if beta < NHEADS // 2:
                        nxt = chain(gen_kvq(beta), gen_stab(h + 1))
                        merge(g2, nxt, ratio=1)
                    else:
                        drain(g2)
            drain(gen_norm(NHEADS - 1))

        P.fence()
        x1bf = E(nc.sbuf_tensor("x1bf", [128, 8 * T], BF16))
        out_acc = E(nc.sbuf_tensor("out_acc", [128, NTILE * D], F32))
        combine = E(nc.sbuf_tensor("combine", [128, 128], F32))
        wob = s1.enter_context(nc.sbuf_tensor("wob", [128, 8 * D], BF16, side="right"))
        gld(wob[:].rearrange("p (hc d) -> p hc d", hc=8),
            wod.rearrange("(hc p) d -> p hc d", p=128), "wob", "ld_wo")
        P.op("gp", lambda g: g.dma_start(out=dummy[0:1, 0:32], in_=ident[0:1, 0:32]),
             writes=["dummy_tail"], tag="gp_pad_wo")
        with ExitStack() as wctx:
            W = wctx.enter_context
            xot = [W(nc.sbuf_tensor(f"xot{i}", [128, D], F32)) for i in range(3)]
            t1 = [W(nc.sbuf_tensor(f"t1_{i}", [128, D], F32)) for i in range(2)]
            x1tmb_ = [W(nc.sbuf_tensor(f"x1tmb{i}", [128, D], BF16)) for i in range(2)]
            psw = [W(nc.psum_tensor(f"psw{i}", [128, 512], F32)) for i in range(4)]
            psb = [W(nc.psum_tensor(f"psb{i}", [128, 512], BF16)) for i in range(2)]

            def ld_xot(tt):
                sld(xot[tt % 3][:],
                    xtm.rearrange("(j p) d -> p j d", p=128)[:, tt, :],
                    f"xot{tt % 3}", f"ld_xo{tt}")

            ld_xot(0)
            ld_xot(1)
            pending_xtr = [None]
            for tt in range(NTILE):
                if tt + 2 < NTILE:
                    ld_xot(tt + 2)
                elif tt == NTILE - 2:
                    P.op("sp", lambda sx: sx.dma_start(
                        out=dummy[0:1, 32:64], in_=ident[0:1, 32:64]),
                        writes=["dummy_sp"], tag="sp_pad_wo")
                xbuf = xot[tt % 3]
                xbname = f"xot{tt % 3}"
                tbuf = t1[tt % 2]
                tbname = f"t1_{tt % 2}"
                for dh in range(2):
                    bank = psw[(2 * tt + dh) % 4]
                    bname = f"psw{(2 * tt + dh) % 4}"

                    def womm(t, bank=bank, tt=tt, dh=dh):
                        for hc in range(8):
                            ins = t.matmul(
                                bank[:],
                                aall[:, hc * T + tt * 128: hc * T + tt * 128 + 128],
                                wob[:, hc * D + dh * 512: hc * D + dh * 512 + 512],
                                start=(hc == 0), stop=(hc == 7))
                        return ins
                    P.op("pe", womm, reads=["aall", "wob"], writes=[bname],
                         tag=f"womm{tt}.{dh}")

                    def wadd(v, bank=bank, tbuf=tbuf, xbuf=xbuf, dh=dh):
                        return v.tensor_tensor(tbuf[:, dh * 512:(dh + 1) * 512],
                                               bank[:], xbuf[:, dh * 512:(dh + 1) * 512],
                                               ALU.add)
                    P.op("dve", wadd, reads=[bname, xbname], writes=[tbname],
                         tag=f"wadd{tt}.{dh}")

                if pending_xtr[0] is not None:
                    pending_xtr[0]()
                    pending_xtr[0] = None

                sc_buf = t1[(tt + 1) % 2]
                sc_name = f"t1_{(tt + 1) % 2}"

                P.op("dve", lambda v, tbuf=tbuf: v.reduce_sum(st_s1, tbuf[:], axis=AX.X),
                     reads=[tbname], writes=["smallA"], force_sync=True, tag=f"ln1s1_{tt}")
                P.op("dve", lambda v: v.tensor_scalar(st_nm, st_s1, -1.0 / D, None,
                                                      ALU.mult),
                     reads=["smallA"], writes=["smallA"], force_sync=True,
                     tag=f"ln1nm_{tt}")
                P.op("dve", lambda v, tbuf=tbuf, sc_buf=sc_buf: v.tensor_scalar(
                    sc_buf[:], tbuf[:], st_nm, None, ALU.add),
                    reads=[tbname, "smallA"], writes=[sc_name], force_sync=True,
                    tag=f"ln1ctr_{tt}")
                P.op("dve", lambda v, sc_buf=sc_buf: v.tensor_tensor(
                    sc_buf[:], sc_buf[:], sc_buf[:], ALU.mult),
                    reads=[sc_name], writes=[sc_name], force_sync=True,
                    tag=f"ln1sq_{tt}")
                P.op("dve", lambda v, sc_buf=sc_buf: v.reduce_sum(
                    st_s2, sc_buf[:], axis=AX.X),
                    reads=[sc_name], writes=["smallA"], force_sync=True,
                    tag=f"ln1s2_{tt}")
                P.op("dve", lambda v: v.tensor_scalar(st_var, st_s2, 1.0 / D, None,
                                                      ALU.mult),
                     reads=["smallA"], writes=["smallA"], force_sync=True,
                     tag=f"ln1var_{tt}")
                P.op("act", lambda sc: sc.activation(st_l, st_var, AF.Ln, bias=st_eps),
                     reads=["smallA"], writes=["smallA"], force_sync=True,
                     tag=f"ln1l_{tt}")
                P.op("act", lambda sc: sc.activation(st_rstd, st_l, AF.Exp, scale=-0.5),
                     reads=["smallA"], writes=["smallA"], force_sync=True,
                     tag=f"ln1r_{tt}")
                P.op("dve", lambda v: v.tensor_scalar(st_rstd2, st_rstd, 2.0, None,
                                                      ALU.mult),
                     reads=["smallA"], writes=["smallA"], force_sync=True,
                     tag=f"ln1r2_{tt}")
                P.op("dve", lambda v, tbuf=tbuf, tt=tt: v.tensor_scalar(
                    out_acc[:, tt * D:(tt + 1) * D], tbuf[:], st_nm, st_rstd2,
                    ALU.add, ALU.mult),
                    reads=[tbname, "smallA"], writes=[f"oacc{tt}"], force_sync=True,
                    tag=f"ln1acc_{tt}")
                x1tmb = x1tmb_[tt % 2]
                x1name = f"x1tmb{tt % 2}"
                P.op("dve", lambda v, tbuf=tbuf, x1tmb=x1tmb: v.tensor_scalar(
                    x1tmb[:], tbuf[:], st_nm, st_rstd, ALU.add, ALU.mult),
                    reads=[tbname, "smallA"], writes=[x1name], force_sync=True,
                    tag=f"ln1x1_{tt}")
                def emit_xtr(tt=tt, x1tmb=x1tmb, x1name=x1name):
                    for half in range(2):
                        bank = psb[half]
                        bname = f"psb{half}"

                        def xtr(t, bank=bank, half=half, x1tmb=x1tmb):
                            for q in range(4):
                                ins = t.transpose(
                                    bank[:, q * 128:(q + 1) * 128],
                                    x1tmb[:, half * 512 + q * 128: half * 512 + (q + 1) * 128],
                                    idnb[:])
                            return ins
                        P.op("pe", xtr, reads=[x1name, "idnb"], writes=[bname],
                             tag=f"xtr{tt}.{half}")

                        def xev(sc, bank=bank, tt=tt, half=half):
                            return sc.activation(
                                x1bf[:].rearrange("p (dc t) -> p dc t", dc=8)
                                [:, half * 4:(half + 1) * 4, tt * 128:(tt + 1) * 128],
                                bank[:].rearrange("p (q c) -> p q c", q=4),
                                AF.Copy)
                        P.op("act", xev, reads=[bname], writes=["x1bf"],
                             tag=f"xev{tt}.{half}")
                pending_xtr[0] = emit_xtr
            pending_xtr[0]()
            pending_xtr[0] = None

        s1.close()
        P.fence()
        with ExitStack() as rctx:
            Rr = rctx.enter_context
            centb = Rr(nc.sbuf_tensor("centb", [128, 8 * 8], BF16))
            rbs = Rr(nc.sbuf_tensor("rbs", [128, 128], F32))
            rtmp = Rr(nc.sbuf_tensor("rtmp", [128, 6 * 128 + 64], F32))
            psr = Rr(nc.psum_tensor("psr", [128, 512], F32))

            bal = rtmp[:, 0:128]
            scr = rtmp[:, 128:256]
            rep = rtmp[:, 256:384]
            qtmp = rtmp[:, 384:512]
            msk1 = rtmp[:, 512:640]
            msk2 = rtmp[:, 640:768]
            g1 = rtmp[:, 768:784]
            g2 = rtmp[:, 784:800]
            w1 = rtmp[:, 800:816]
            w2 = rtmp[:, 816:832]

            gld(centb[:].rearrange("p (dc c) -> p dc c", dc=8),
                cent.rearrange("(dc p) c -> p dc c", p=128), "centb", "ld_cent")
            P.op("gp", lambda g: g.dma_start(out=dummy[0:1, 0:32], in_=ident[0:1, 0:32]),
                 writes=["dummy_tail"], tag="gp_pad_rt")
            sld(rbs[:], rbias, "rbs", "ld_rbias")
            P.op("sp", lambda sx: sx.dma_start(
                out=dummy[0:1, 32:64], in_=ident[0:1, 32:64]),
                writes=["dummy_sp"], tag="sp_pad_rt")

            for tt in range(NTILE):
                def rmm(t, tt=tt):
                    for dc in range(8):
                        ins = t.matmul(
                            psr[:, tt * 8:(tt + 1) * 8],
                            x1bf[:, dc * T + tt * 128: dc * T + tt * 128 + 128],
                            centb[:, dc * 8:(dc + 1) * 8],
                            start=(tt == 0 and dc == 0),
                            stop=(tt == NTILE - 1 and dc == 7))
                    return ins
                P.op("pe", rmm, reads=["x1bf", "centb"], writes=["psr"],
                     tag=f"rmm{tt}")
            P.op("act", lambda sc: sc.activation(scr, psr[:, 0:128], AF.Sigmoid),
                 reads=["psr"], writes=["rtmp"], tag="sig_scr")
            P.op("dve", lambda v: v.tensor_tensor(qtmp, psr[:, 0:128], rbs[:],
                                                  ALU.add),
                 reads=["psr", "rbs"], writes=["rtmp"], tag="rawb")
            P.op("act", lambda sc: sc.activation(bal, qtmp, AF.Sigmoid),
                 reads=["rtmp"], writes=["rtmp"], force_sync=True, tag="sig_bal")

            def seg(ap):
                return ap.rearrange("p (s e) -> p s e", e=8)

            def col3(ap):
                return ap.rearrange("p (s one) -> p s one", one=1)

            def dop(fn, tag):
                P.op("dve", fn, reads=["rtmp"], writes=["rtmp"], force_sync=True,
                     tag=tag)

            dop(lambda v: v.reduce_max(col3(g1), seg(bal), axis=AX.X), "r_m1")
            for e in range(8):
                dop(lambda v, e=e: v.tensor_copy(
                    seg(rep)[:, :, e:e + 1], col3(g1)), f"r_rep1{e}")
            dop(lambda v: v.tensor_tensor(msk1, bal, rep, ALU.is_ge), "r_msk1")
            dop(lambda v: v.scalar_tensor_tensor(qtmp, msk1, -1e30, bal,
                                                 ALU.mult, ALU.add), "r_balx")
            dop(lambda v: v.reduce_max(col3(g2), seg(qtmp), axis=AX.X), "r_m2")
            for e in range(8):
                dop(lambda v, e=e: v.tensor_copy(
                    seg(rep)[:, :, e:e + 1], col3(g2)), f"r_rep2{e}")
            dop(lambda v: v.tensor_tensor(msk2, qtmp, rep, ALU.is_ge), "r_msk2")
            dop(lambda v: v.tensor_tensor(qtmp, msk1, scr, ALU.mult), "r_q1")
            dop(lambda v: v.reduce_sum(col3(g1), seg(qtmp), axis=AX.X), "r_g1")
            dop(lambda v: v.tensor_tensor(qtmp, msk2, scr, ALU.mult), "r_q2")
            dop(lambda v: v.reduce_sum(col3(g2), seg(qtmp), axis=AX.X), "r_g2")
            dop(lambda v: v.tensor_tensor(w1, g1, g2, ALU.subtract), "r_d1")
            P.op("act", lambda sc: sc.activation(w2, w1, AF.Sigmoid, scale=-1.0),
                 reads=["rtmp"], writes=["rtmp"], force_sync=True, tag="r_w2")
            P.op("act", lambda sc: sc.activation(w1, w1, AF.Sigmoid),
                 reads=["rtmp"], writes=["rtmp"], force_sync=True, tag="r_w1")
            for e in range(8):
                dop(lambda v, e=e: v.tensor_copy(
                    seg(rep)[:, :, e:e + 1], col3(w1)), f"r_repw1{e}")
            dop(lambda v: v.tensor_tensor(qtmp, msk1, rep, ALU.mult), "r_c1")
            for e in range(8):
                dop(lambda v, e=e: v.tensor_copy(
                    seg(rep)[:, :, e:e + 1], col3(w2)), f"r_repw2{e}")
            dop(lambda v: v.tensor_tensor(rep, msk2, rep, ALU.mult), "r_c2")
            P.op("dve", lambda v: v.tensor_tensor(combine[:], qtmp, rep, ALU.add),
                 reads=["rtmp"], writes=["combine"], force_sync=True, tag="r_comb")

        ysb = [E(nc.sbuf_tensor(f"ysb{i}", [128, D], F32)) for i in range(2)]
        sqb = E(nc.sbuf_tensor("sqb", [128, D], F32))

        def emit_ln2(tt):
            src = out_acc[:, tt * D:(tt + 1) * D]
            oan = f"oacc{tt}"
            P.op("dve", lambda v, src=src: v.reduce_sum(st_s1, src, axis=AX.X),
                 reads=[oan], writes=["smallA"], force_sync=True,
                 tag=f"ln2s1_{tt}")
            P.op("dve", lambda v: v.tensor_scalar(st_nm, st_s1, -1.0 / D, None,
                                                  ALU.mult),
                 reads=["smallA"], writes=["smallA"], force_sync=True,
                 tag=f"ln2nm_{tt}")
            P.op("dve", lambda v, src=src: v.tensor_scalar(
                sqb[:], src, st_nm, None, ALU.add),
                reads=[oan, "smallA"], writes=["sqb"], force_sync=True,
                tag=f"ln2ctr_{tt}")
            P.op("dve", lambda v: v.tensor_tensor(sqb[:], sqb[:], sqb[:],
                                                  ALU.mult),
                 reads=["sqb"], writes=["sqb"], force_sync=True,
                 tag=f"ln2sq_{tt}")
            P.op("dve", lambda v: v.reduce_sum(st_s2, sqb[:], axis=AX.X),
                 reads=["sqb"], writes=["smallA"], force_sync=True,
                 tag=f"ln2s2_{tt}")
            P.op("dve", lambda v: v.tensor_scalar(st_var, st_s2, 1.0 / D, None,
                                                  ALU.mult),
                 reads=["smallA"], writes=["smallA"], force_sync=True,
                 tag=f"ln2var_{tt}")
            P.op("act", lambda sc: sc.activation(st_l, st_var, AF.Ln, bias=st_eps),
                 reads=["smallA"], writes=["smallA"], force_sync=True,
                 tag=f"ln2l_{tt}")
            P.op("act", lambda sc: sc.activation(st_rstd, st_l, AF.Exp, scale=-0.5),
                 reads=["smallA"], writes=["smallA"], force_sync=True,
                 tag=f"ln2r_{tt}")
            ybuf = ysb[tt % 2]
            ybname = f"ysb{tt % 2}"
            P.op("dve", lambda v, src=src, ybuf=ybuf: v.tensor_scalar(
                ybuf[:], src, st_nm, st_rstd, ALU.add, ALU.mult),
                reads=[oan, "smallA"], writes=[ybname], force_sync=True,
                tag=f"ln2y_{tt}")
            P.op("dve", lambda v, ybuf=ybuf: v.tensor_copy(ybuf[:, 0:1],
                                                           ybuf[:, 0:1]),
                 reads=[ybname], writes=[ybname], force_sync=True,
                 tag=f"ln2settle_{tt}")
            P.op("sp", lambda s, ybuf=ybuf, tt=tt: s.dma_start(
                out=y.rearrange("(j p) d -> p j d", p=128)[:, tt, :],
                in_=ybuf[:]),
                reads=[ybname], writes=["y"], tag=f"st{tt}")

        P.fence()
        with ExitStack() as fctx:
            Ff = fctx.enter_context
            wfcb = [Ff(nc.sbuf_tensor(f"wfcb{i}", [128, 8 * 1024], BF16))
                    for i in range(2)]
            wprb = [Ff(nc.sbuf_tensor(f"wprb{i}", [128, 8 * 1024], BF16))
                    for i in range(2)]
            hb = Ff(nc.sbuf_tensor("hb", [128, 8 * T], BF16))
            psf = [Ff(nc.psum_tensor(f"psf{i}", [128, 512], F32)) for i in range(2)]
            psp = [Ff(nc.psum_tensor(f"psp{i}", [128, 512], F32)) for i in range(3)]

            NUNIT = 4
            units = [(e, u) for e in range(NE_R + NE_S) for u in range(NUNIT)]

            def load_unit(idx):
                e, u = units[idx]
                slot = idx % 2
                gld(wfcb[slot][:].rearrange("p (dc f) -> p dc f", dc=8),
                    wfc[e].rearrange("(dc p) f -> p dc f", p=128)
                    [:, :, u * 1024:(u + 1) * 1024],
                    f"wfcb{slot}", f"ldfc{idx}")
                gld(wprb[slot][:].rearrange("p (fc d) -> p fc d", fc=8),
                    wpr[e].rearrange("(fc p) d -> p fc d", p=128)
                    [:, u * 8:(u + 1) * 8, :],
                    f"wprb{slot}", f"ldpr{idx}")

            load_unit(0)
            load_unit(1)
            fcg = [0]
            prg = [0]
            for idx, (e, u) in enumerate(units):
                slot = idx % 2
                wf, wp = wfcb[slot], wprb[slot]
                wfn, wpn = f"wfcb{slot}", f"wprb{slot}"
                for Ft in range(8):
                    for tc in range(4):
                        bank = psf[fcg[0] % 2]
                        bname = f"psf{fcg[0] % 2}"
                        fcg[0] += 1

                        def fcmm(t, bank=bank, wf=wf, Ft=Ft, tc=tc):
                            for dc in range(8):
                                ins = t.matmul(
                                    bank[:],
                                    wf[:, dc * 1024 + Ft * 128: dc * 1024 + (Ft + 1) * 128],
                                    x1bf[:, dc * T + tc * 512: dc * T + tc * 512 + 512],
                                    start=(dc == 0), stop=(dc == 7))
                            return ins
                        P.op("pe", fcmm, reads=[wfn, "x1bf"], writes=[bname],
                             tag=f"fc{idx}.{Ft}.{tc}")

                        def gel(sc, bank=bank, Ft=Ft, tc=tc):
                            return sc.activation(
                                hb[:, Ft * T + tc * 512: Ft * T + tc * 512 + 512],
                                bank[:], AF.Gelu)
                        P.op("act", gel, reads=[bname], writes=["hb"],
                             tag=f"gel{idx}.{Ft}.{tc}")
                for tt in range(NTILE):
                    for dh in range(2):
                        bank = psp[prg[0] % 3]
                        bname = f"psp{prg[0] % 3}"
                        prg[0] += 1

                        def prmm(t, bank=bank, wp=wp, tt=tt, dh=dh):
                            for fc_ in range(8):
                                ins = t.matmul(
                                    bank[:],
                                    hb[:, fc_ * T + tt * 128: fc_ * T + tt * 128 + 128],
                                    wp[:, fc_ * 1024 + dh * 512: fc_ * 1024 + dh * 512 + 512],
                                    start=(fc_ == 0), stop=(fc_ == 7))
                            return ins
                        P.op("pe", prmm, reads=["hb", wpn], writes=[bname],
                             tag=f"pr{idx}.{tt}.{dh}")

                        if e < NE_R:
                            def ev(v, bank=bank, tt=tt, dh=dh, e=e):
                                dst = out_acc[:, tt * D + dh * 512:
                                              tt * D + dh * 512 + 512]
                                return v.scalar_tensor_tensor(
                                    dst, bank[:], combine[:, tt * 8 + e:tt * 8 + e + 1],
                                    dst, ALU.mult, ALU.add)
                            P.op("dve", ev, reads=[bname, "combine", f"oacc{tt}"],
                                 writes=[f"oacc{tt}"], tag=f"ev{idx}.{tt}.{dh}")
                        else:
                            def ev(v, bank=bank, tt=tt, dh=dh):
                                dst = out_acc[:, tt * D + dh * 512:
                                              tt * D + dh * 512 + 512]
                                return v.tensor_tensor(dst, bank[:], dst, ALU.add)
                            P.op("dve", ev, reads=[bname, f"oacc{tt}"],
                                 writes=[f"oacc{tt}"], tag=f"ev{idx}.{tt}.{dh}")
                        if idx == len(units) - 1 and dh == 1:
                            emit_ln2(tt)
                if idx + 2 < len(units):
                    load_unit(idx + 2)
                    if idx + 2 == len(units) - 1:
                        P.op("gp", lambda g: g.dma_start(
                            out=dummy[0:1, 0:32], in_=ident[0:1, 0:32]),
                            writes=["dummy_tail"], tag="gp_pad_ffn")

        sp_final = P.counts["sp"]

        @block.tensor
        def _(t):
            P.emit_engine("pe", t)

        @block.scalar
        def _(sc):
            P.emit_engine("act", sc)

        @block.vector
        def _(v):
            P.emit_engine("dve", v)

        @block.sync
        def _(s):
            P.emit_engine("sp", s)
            s.wait_ge(P.sems["sp"], sp_final)

        @block.gpsimd
        def _(g):
            P.emit_engine("gp", g)

    return nc


_NC_CACHE = None


def kernel(L0_attn_x, L0_attn_WQ, L0_attn_WK, L0_attn_WV, L0_attn_WO,
           L0_ffn_centroids, L0_ffn_routing_bias, L0_ffn_Wfc_r, L0_ffn_Wproj_r,
           L0_ffn_Wfc_sh, L0_ffn_Wproj_sh):
    global _NC_CACHE
    x = np.asarray(L0_attn_x, np.float32)

    wq = np.ascontiguousarray(np.asarray(L0_attn_WQ, np.float32)
                              .transpose(2, 0, 1).reshape(D, D))
    wk = np.ascontiguousarray(np.asarray(L0_attn_WK, np.float32)
                              .transpose(2, 0, 1).reshape(D, D))
    wv = np.ascontiguousarray(np.asarray(L0_attn_WV, np.float32)
                              .transpose(2, 0, 1).reshape(D, D))
    wo = np.ascontiguousarray(np.asarray(L0_attn_WO, np.float32)
                              .transpose(0, 2, 1).reshape(D, D))
    cent = np.ascontiguousarray(np.asarray(L0_ffn_centroids, np.float32).T)
    rbias = np.ascontiguousarray(np.broadcast_to(
        np.tile(np.asarray(L0_ffn_routing_bias, np.float32), NTILE)[None, :],
        (128, 128)))
    wfc = np.ascontiguousarray(np.concatenate(
        [np.asarray(L0_ffn_Wfc_r, np.float32),
         np.asarray(L0_ffn_Wfc_sh, np.float32)], axis=0).transpose(0, 2, 1))
    wpr = np.ascontiguousarray(np.concatenate(
        [np.asarray(L0_ffn_Wproj_r, np.float32),
         np.asarray(L0_ffn_Wproj_sh, np.float32)], axis=0).transpose(0, 2, 1))
    ident = np.eye(128, dtype=np.float32)
    identb = np.eye(128, dtype=ml_dtypes.bfloat16)

    xts = [np.ascontiguousarray(x[b].T) for b in range(2)]

    in_maps = []
    for c in range(8):
        b, k = c // 4, c % 4
        rows = np.concatenate(
            [np.arange(128) + 512 * j + 128 * k for j in range(NTILE)])
        xtm_c = np.ascontiguousarray(x[b][rows, :])
        xt_own = np.ascontiguousarray(xts[b][:, rows])
        mask = np.zeros((4, 128, 128), np.float32)
        for bb in range(4):
            kk = np.arange(128)[:, None]
            rr = np.arange(128)[None, :]
            mask[bb] = np.where(128 * bb + kk <= 128 * k + rr, 0.0, -1e30)
        in_maps.append(dict(
            xt=xts[b], xtm=xtm_c, xt_own=xt_own,
            wq=wq, wk=wk, wv=wv, wo=wo, cent=cent, rbias=rbias,
            wfc=wfc, wpr=wpr,
            mask01=mask.astype(ml_dtypes.bfloat16), ident=ident, identb=identb,
        ))

    if _NC_CACHE is None:
        _NC_CACHE = build_program()
    res = run_bass_kernel_spmd(_NC_CACHE, in_maps, core_ids=list(range(8)))

    out = np.empty((2, S, D), np.float32)
    for c in range(8):
        b, k = c // 4, c % 4
        yc = np.asarray(res.results[c]["y"], np.float32)
        for j in range(NTILE):
            out[b, 512 * j + 128 * k: 512 * j + 128 * k + 128, :] = \
                yc[j * 128:(j + 1) * 128, :]
    return out


# version 14
# speedup vs baseline: 1.7960x; 1.0042x over previous
import numpy as np
import ml_dtypes
from collections import defaultdict
from contextlib import ExitStack

import concourse.bass as bass
import concourse.mybir as mybir
from concourse.bass_utils import run_bass_kernel_spmd

F32 = mybir.dt.float32
BF16 = mybir.dt.bfloat16
AF = mybir.ActivationFunctionType
ALU = mybir.AluOpType
AX = mybir.AxisListType

S = 8192
T = 2048
D = 1024
NHEADS = 16
KC = 64
NTILE = 16
NE_R = 8
NE_S = 2
F_FFN = 4096
NEG = -3.0e38
EPS = 1e-5


class _Res:
    __slots__ = ("writer", "readers")

    def __init__(self):
        self.writer = None
        self.readers = []


class Plan:
    ENGINES = {"pe": 1, "act": 1, "dve": 1, "sp": 16, "gp": 16}

    def __init__(self, nc, ctx):
        self.nc = nc
        self.sems = {}
        self.counts = {}
        self.ops = {e: [] for e in self.ENGINES}
        self.satisfied = {e: defaultdict(int) for e in self.ENGINES}
        for e in self.ENGINES:
            self.sems[e] = ctx.enter_context(nc.semaphore(f"sem_{e}"))
            self.counts[e] = 0
        self._res = {}
        self._fence = None
        self._fence_applied = set()
        self._gseq = 0
        self._dma_seq = {"gp": {}, "sp": {}}

    def res(self, name):
        r = self._res.get(name)
        if r is None:
            r = _Res()
            self._res[name] = r
        return r

    def fence(self):
        self._fence = [(e, self.counts[e]) for e in self.ENGINES if self.counts[e]]
        self._fence_applied = set()

    def op(self, eng, emit, reads=(), writes=(), extra_waits=(), force_sync=False,
           tag=""):
        reads = [self.res(r) if isinstance(r, str) else r for r in reads]
        writes = [self.res(r) if isinstance(r, str) else r for r in writes]
        want = {}

        def need(tok):
            if tok is None:
                return
            s, c = tok
            if want.get(s, 0) < c:
                want[s] = c

        for r in reads:
            need(r.writer)
        for r in writes:
            need(r.writer)
            for t in r.readers:
                need(t)
        for t in extra_waits:
            need(t)
        if self._fence is not None and eng not in self._fence_applied:
            for t in self._fence:
                need(t)
            self._fence_applied.add(eng)
            if len(self._fence_applied) == len(self.ENGINES):
                self._fence = None

        waits = []
        sat = self.satisfied[eng]
        for s, c in sorted(want.items()):
            if s == eng and not force_sync and c <= self.counts[eng]:
                continue
            if s == eng and force_sync and c <= self.counts[eng]:
                if sat[s] >= c:
                    continue
                waits.append((s, c))
                sat[s] = c
                continue
            if sat[s] >= c:
                continue
            waits.append((s, c))
            sat[s] = c

        step = self.ENGINES[eng]
        self.counts[eng] += step
        tok = (eng, self.counts[eng])
        seq = self._gseq
        self._gseq += 1
        if step == 16:
            self._dma_seq[eng][self.counts[eng]] = seq
        self.ops[eng].append((waits, emit, step, tag, seq))
        for r in reads:
            r.readers.append(tok)
        for r in writes:
            r.writer = tok
            r.readers = []
        return tok

    def emit_engine(self, eng, obj):
        sems = self.sems
        for waits, emit, step, tag, seq in self.ops[eng]:
            for s, c in waits:
                if s in ("gp", "sp"):
                    nseq = self._dma_seq[s].get(c + 16)
                    if nseq is not None and nseq < seq:
                        c = c + 16
                obj.wait_ge(sems[s], c)
            last = emit(obj)
            assert last is not None, f"emit {tag!r} on {eng} returned None"
            last.then_inc(sems[eng], step)


def build_program():
    nc = bass.Bass("TRN2", target_bir_lowering=False, debug=False)

    xt = nc.dram_tensor("xt", [D, S], F32, kind="ExternalInput").ap()
    xto = nc.dram_tensor("xt_own", [D, T], F32, kind="ExternalInput").ap()
    xtm = nc.dram_tensor("xtm", [T, D], F32, kind="ExternalInput").ap()
    wqd = nc.dram_tensor("wq", [D, D], F32, kind="ExternalInput").ap()
    wkd = nc.dram_tensor("wk", [D, D], F32, kind="ExternalInput").ap()
    wvd = nc.dram_tensor("wv", [D, D], F32, kind="ExternalInput").ap()
    wod = nc.dram_tensor("wo", [D, D], F32, kind="ExternalInput").ap()
    cent = nc.dram_tensor("cent", [D, 8], F32, kind="ExternalInput").ap()
    rbias = nc.dram_tensor("rbias", [128, 128], F32, kind="ExternalInput").ap()
    wfc = nc.dram_tensor("wfc", [NE_R + NE_S, D, F_FFN], F32, kind="ExternalInput").ap()
    wpr = nc.dram_tensor("wpr", [NE_R + NE_S, F_FFN, D], F32, kind="ExternalInput").ap()
    mask01 = nc.dram_tensor("mask01", [4, 128, 128], BF16, kind="ExternalInput").ap()
    ident = nc.dram_tensor("ident", [128, 128], F32, kind="ExternalInput").ap()
    identb = nc.dram_tensor("identb", [128, 128], BF16, kind="ExternalInput").ap()
    y = nc.dram_tensor("y", [T, D], F32, kind="ExternalOutput").ap()
    x16 = nc.dram_tensor("x16s", [D, S], BF16, kind="Internal").ap()
    x16q = nc.dram_tensor("x16q", [D, T], BF16, kind="Internal").ap()

    with ExitStack() as ctx:
        E = ctx.enter_context
        P = Plan(nc, ctx)
        block = E(nc.Block())

        m01 = E(nc.sbuf_tensor("m01", [128, 4 * 128], BF16))
        idn = E(nc.sbuf_tensor("idn", [128, 128], F32))
        idnb = E(nc.sbuf_tensor("idnb", [128, 128], BF16))
        ones1 = E(nc.sbuf_tensor("ones1", [1, 128], F32))
        smallA = E(nc.sbuf_tensor("smallA", [128, 64], F32))
        dummy = E(nc.sbuf_tensor("dummy_tail", [1, 64], F32))
        s1 = ExitStack()
        aall = s1.enter_context(nc.sbuf_tensor("aall", [128, 8 * T], BF16, side="right"))

        strip = smallA[:, 0:16]
        negm = smallA[:, 16:32]
        st_s1 = smallA[:, 32:33]
        st_nm = smallA[:, 33:34]
        st_s2 = smallA[:, 34:35]
        st_var = smallA[:, 35:36]
        st_l = smallA[:, 36:37]
        st_rstd = smallA[:, 37:38]
        st_rstd2 = smallA[:, 38:39]
        st_eps = smallA[:, 39:40]

        def gld(dst, src, name, tag):
            P.op("gp", lambda g, dst=dst, src=src: g.dma_start(out=dst, in_=src),
                 writes=[name], tag=tag)

        def sld(dst, src, name, tag):
            P.op("sp", lambda s, dst=dst, src=src: s.dma_start(out=dst, in_=src),
                 writes=[name], tag=tag)

        sld(m01[:].rearrange("p (b c) -> p b c", b=4),
            mask01.rearrange("b k c -> k b c"), "m01", "ld_m01")
        sld(idn[:], ident, "idn", "ld_idn")
        sld(idnb[:], identb, "idnb", "ld_idnb")
        P.op("dve", lambda v: v.memset(ones1[:], 1.0), writes=["ones1"], tag="ones")

        with ExitStack() as actx:
            A = actx.enter_context
            qa = [A(nc.sbuf_tensor(f"qa{i}", [65, T], BF16)) for i in range(4)]
            rsig = A(nc.sbuf_tensor("rsig", [1, 512], F32))
            lsig = A(nc.sbuf_tensor("lsig", [1, 512], F32))
            bcs = A(nc.sbuf_tensor("bcs", [64, 512], F32))

            ps = [A(nc.psum_tensor(f"ps{i}", [128, 512], F32)) for i in range(8)]

            wqb = A(nc.sbuf_tensor("wqb", [128, 8 * D], BF16))
            wkb = A(nc.sbuf_tensor("wkb", [128, 8 * D], BF16))
            wvb = A(nc.sbuf_tensor("wvb", [128, 8 * D], BF16))
            ka = [A(nc.sbuf_tensor(f"ka{i}", [65, S], BF16)) for i in range(3)]
            va = [A(nc.sbuf_tensor(f"va{i}", [128, KC * 65], BF16)) for i in range(3)]
            xc = [A(nc.sbuf_tensor(f"xc{i}", [128, 8 * 256], BF16)) for i in range(3)]
            eb = [A(nc.sbuf_tensor(f"eb{i}", [128, T], BF16)) for i in range(4)]
            gld(wqb[:].rearrange("p (dc c) -> p dc c", dc=8),
                wqd.rearrange("(dc p) c -> p dc c", p=128), "wqb", "ld_wq")
            gld(wkb[:].rearrange("p (dc c) -> p dc c", dc=8),
                wkd.rearrange("(dc p) c -> p dc c", p=128), "wkb", "ld_wk")
            gld(wvb[:].rearrange("p (dc c) -> p dc c", dc=8),
                wvd.rearrange("(dc p) c -> p dc c", p=128), "wvb", "ld_wv")

            def init_attn(v):
                for i in range(3):
                    v.memset(va[i][:].rearrange("p (kc c) -> p kc c", c=65)[:, :, 64:65], 1.0)
                    v.memset(ka[i][64:65, :], 1.0)
                for i in range(4):
                    v.memset(qa[i][64:65, :], 0.0)
                v.memset(smallA[:], 0.0)
                return v.memset(st_eps, EPS)
            P.op("dve", init_attn, writes=["init_scratch"], tag="init_attn_a")
            P.op("dve", init_attn,
                 writes=["smallA"] + [f"va{i}" for i in range(3)] + [f"ka{i}" for i in range(3)],
                 tag="init_attn")

            rot01 = [0]

            def gen_kvq(beta):
                pq = beta
                sk = [(2 * beta) % 3, (2 * beta + 1) % 3]
                sq = [(2 * beta) % 4, (2 * beta + 1) % 4]

                def emit_load(i):
                    if i >= 40:
                        P.op("gp", lambda g: g.dma_start(
                            out=dummy[0:1, 0:32], in_=ident[0:1, 0:32]),
                            writes=["dummy_tail"], tag=f"gp_pad{beta}")
                        P.op("sp", lambda s: s.dma_start(
                            out=dummy[0:1, 32:64], in_=ident[0:1, 32:64]),
                            writes=["dummy_sp"], tag=f"sp_pad{beta}")
                        return
                    buf, bufname = xc[i % 3], f"xc{i % 3}"
                    if i < 32:
                        src32 = xt.rearrange("(dc p) t -> p dc t", p=128)[
                            :, :, i * 256:(i + 1) * 256]
                        s16 = x16.rearrange("(dc p) t -> p dc t", p=128)[
                            :, :, i * 256:(i + 1) * 256]
                        nm = f"x16.{i}"
                    else:
                        qc = i - 32
                        src32 = xto.rearrange("(dc p) t -> p dc t", p=128)[
                            :, :, qc * 256:(qc + 1) * 256]
                        s16 = x16q.rearrange("(dc p) t -> p dc t", p=128)[
                            :, :, qc * 256:(qc + 1) * 256]
                        nm = f"x16q.{i}"
                    bview = buf[:].rearrange("p (dc c) -> p dc c", dc=8)
                    if beta == 0:
                        gld(bview, src32, bufname, f"ld_x{beta}.{i}")
                        P.op("sp", lambda sx, s16=s16, bview=bview: sx.dma_start(
                            out=s16, in_=bview),
                            reads=[bufname], writes=[nm], tag=f"st_x16.{i}")
                    else:
                        P.op("sp", lambda sx, s16=s16, bview=bview: sx.dma_start(
                            out=bview, in_=s16),
                            reads=[nm], writes=[bufname], tag=f"ld16_{beta}.{i}")

                first = [True]
                for ck in range(32):
                    def unit(ck=ck):
                        if first[0]:
                            emit_load(0)
                            emit_load(1)
                            first[0] = False
                        emit_load(ck + 2)
                        buf = xc[ck % 3]
                        bufname = f"xc{ck % 3}"
                        bank = ps[rot01[0] % 2]
                        bname = f"ps{rot01[0] % 2}"
                        rot01[0] += 1

                        def pe_k(t, bank=bank, buf=buf):
                            for dc in range(8):
                                ins = t.matmul(
                                    bank[:, 0:256],
                                    wkb[:, dc * D + pq * 128: dc * D + pq * 128 + 128],
                                    buf[:, dc * 256:(dc + 1) * 256],
                                    start=(dc == 0), stop=(dc == 7))
                            return ins
                        P.op("pe", pe_k, reads=["wkb", bufname], writes=[bname],
                             tag=f"Kmm{beta}.{ck}")

                        def a_k(v, bank=bank, ck=ck):
                            v.tensor_copy(ka[sk[0]][0:64, ck * 256:(ck + 1) * 256],
                                          bank[0:64, 0:256])
                            return v.tensor_copy(
                                ka[sk[1]][0:64, ck * 256:(ck + 1) * 256],
                                bank[64:128, 0:256])
                        P.op("dve", a_k, reads=[bname],
                             writes=[f"ka{sk[0]}", f"ka{sk[1]}"], tag=f"Kev{beta}.{ck}")

                        bank = ps[rot01[0] % 2]
                        bname = f"ps{rot01[0] % 2}"
                        rot01[0] += 1

                        def pe_v(t, bank=bank, buf=buf):
                            ins = None
                            for sub in range(2):
                                for dc in range(8):
                                    ins = t.matmul(
                                        bank[:, sub * 128:(sub + 1) * 128],
                                        buf[:, dc * 256 + sub * 128: dc * 256 + sub * 128 + 128],
                                        wvb[:, dc * D + pq * 128: dc * D + pq * 128 + 128],
                                        start=(sub == 0 and dc == 0),
                                        stop=(sub == 1 and dc == 7))
                            return ins
                        P.op("pe", pe_v, reads=["wvb", bufname], writes=[bname],
                             tag=f"Vmm{beta}.{ck}")

                        def a_v(v, bank=bank, ck=ck):
                            ins = None
                            for sub in range(2):
                                kc = ck * 2 + sub
                                for i in range(2):
                                    ins = v.tensor_copy(
                                        va[sk[i]][:, kc * 65: kc * 65 + 64],
                                        bank[:, sub * 128 + i * 64: sub * 128 + i * 64 + 64])
                            return ins
                        P.op("dve", a_v, reads=[bname],
                             writes=[f"va{sk[0]}", f"va{sk[1]}"], tag=f"Vev{beta}.{ck}")
                    yield unit
                for qc in range(8):
                    def unit(qc=qc):
                        emit_load(32 + qc + 2)
                        buf = xc[(32 + qc) % 3]
                        bufname = f"xc{(32 + qc) % 3}"
                        bank = ps[rot01[0] % 2]
                        bname = f"ps{rot01[0] % 2}"
                        rot01[0] += 1

                        def pe_q(t, bank=bank, buf=buf):
                            for dc in range(8):
                                ins = t.matmul(
                                    bank[:, 0:256],
                                    wqb[:, dc * D + pq * 128: dc * D + pq * 128 + 128],
                                    buf[:, dc * 256:(dc + 1) * 256],
                                    start=(dc == 0), stop=(dc == 7))
                            return ins
                        P.op("pe", pe_q, reads=["wqb", bufname], writes=[bname],
                             tag=f"Qmm{beta}.{qc}")

                        def a_q(v, bank=bank, qc=qc):
                            v.tensor_scalar(qa[sq[0]][0:64, qc * 256:(qc + 1) * 256],
                                            bank[0:64, 0:256], 0.125, None, ALU.mult)
                            return v.tensor_scalar(
                                qa[sq[1]][0:64, qc * 256:(qc + 1) * 256],
                                bank[64:128, 0:256], 0.125, None, ALU.mult)
                        P.op("dve", a_q, reads=[bname],
                             writes=[f"qa{sq[0]}", f"qa{sq[1]}"], tag=f"Qev{beta}.{qc}")
                    yield unit

            mcol = smallA[:, 0:1]
            mval = smallA[0:1, 2:3]

            def gen_stab(h):
                ks = h % 3
                qs = h % 4
                kah, qah = f"ka{ks}", f"qa{qs}"

                def unit_a(h=h, ks=ks, qs=qs):
                    def p1mm(t):
                        return t.matmul(ps[0][:], qa[qs][0:64, 15 * 128:16 * 128],
                                        ka[ks][0:64, 0:512], start=True, stop=True)
                    P.op("pe", p1mm, reads=[kah, qah], writes=["ps0"],
                         tag=f"p1mm{h}")

                    def p1red(v):
                        return v.reduce_max(mcol, ps[0][:, 0:509:4], axis=AX.X)
                    P.op("dve", p1red, reads=["ps0"], writes=["smallA"],
                         tag=f"p1red{h}")
                yield unit_a

                def unit_b(h=h, qs=qs, qah=qah):
                    def mtr(t):
                        return t.transpose(ps[0][0:1, 0:128], mcol, idn[:])
                    P.op("pe", mtr, reads=["smallA", "idn"], writes=["ps0"],
                         tag=f"mtr{h}")

                    def mred(v):
                        return v.reduce_max(mval, ps[0][0:1, 0:128], axis=AX.X,
                                            negate=True)
                    P.op("dve", mred, reads=["ps0"], writes=["smallA"],
                         force_sync=True, tag=f"mred{h}")

                    def mbc(v, qs=qs):
                        return v.tensor_scalar(qa[qs][64:65, :], qa[qs][64:65, :],
                                               0.0, mval, ALU.mult, ALU.add)
                    P.op("dve", mbc, reads=["smallA", qah], writes=[qah],
                         force_sync=True, tag=f"mbc{h}")
                yield unit_b

            def pieces_for(kc):
                c0 = (kc // 4) * 128
                return [(p, max(c0, p * 512), min(T, (p + 1) * 512))
                        for p in range(4) if (p + 1) * 512 > c0]

            def gen_pass2(h):
                ks = h % 3
                qs = h % 4
                kah, qah, vah = f"ka{ks}", f"qa{qs}", f"va{ks}"
                rot = [0]
                for kc in range(KC):
                    ebuf = eb[kc % 4]
                    ebname = f"eb{kc % 4}"
                    c0 = (kc // 4) * 128
                    pcs = pieces_for(kc)

                    def unit_a(h=h, kc=kc, ebuf=ebuf, ebname=ebname, c0=c0,
                               pcs=pcs, ks=ks, qs=qs, kah=kah, qah=qah, rot=rot):
                        for (p, a, b) in pcs:
                            bank = ps[rot[0] % 4]
                            bname = f"ps{rot[0] % 4}"
                            rot[0] += 1

                            def p2mm(t, bank=bank, kc=kc, a=a, b=b):
                                return t.matmul(bank[:, 0:b - a],
                                                ka[ks][0:65, kc * 128:(kc + 1) * 128],
                                                qa[qs][0:65, a:b], start=True, stop=True)
                            P.op("pe", p2mm, reads=[kah, qah], writes=[bname],
                                 tag=f"p2mm{h}.{kc}.{p}")

                            if a == c0:
                                def mska(v, bank=bank, kc=kc, a=a, c0=c0):
                                    return v.tensor_tensor(
                                        bank[:, c0 - a:c0 - a + 128],
                                        bank[:, c0 - a:c0 - a + 128],
                                        m01[:, (kc % 4) * 128:(kc % 4 + 1) * 128],
                                        ALU.add)
                                P.op("dve", mska, reads=[bname, "m01"],
                                     writes=[bname], tag=f"mska{h}.{kc}")

                            def aexp(sc, bank=bank, a=a, b=b):
                                return sc.activation(ebuf[:, a:b], bank[:, 0:b - a],
                                                     AF.Exp)
                            P.op("act", aexp, reads=[bname], writes=[ebname],
                                 tag=f"exp{h}.{kc}.{p}")
                    yield unit_a

                    def unit_b(h=h, kc=kc, ebuf=ebuf, ebname=ebname, pcs=pcs,
                               ks=ks, vah=vah):
                        for (p, a, b) in pcs:
                            pvname = f"ps{4 + p}"

                            def pv(t, kc=kc, p=p, a=a, b=b):
                                return t.matmul(
                                    ps[4 + p][0:65, a - p * 512:b - p * 512],
                                    va[ks][:, kc * 65:(kc + 1) * 65],
                                    ebuf[:, a:b],
                                    start=(kc == 0), stop=(kc == 16 * p + 15))
                            P.op("pe", pv, reads=[vah, ebname], writes=[pvname],
                                 tag=f"pv{h}.{kc}.{p}")
                    yield unit_b

            def gen_norm(h):
                for p in range(4):
                    def unit_rcp(h=h, p=p):
                        def rln(sc, p=p):
                            return sc.activation(lsig[0:1, 0:512],
                                                 ps[4 + p][64:65, 0:512], AF.Ln)
                        P.op("act", rln, reads=[f"ps{4 + p}"], writes=["lsig"],
                             tag=f"rln{h}.{p}")

                        def rex(sc):
                            return sc.activation(rsig[0:1, 0:512], lsig[0:1, 0:512],
                                                 AF.Exp, scale=-1.0)
                        P.op("act", rex, reads=["lsig"], writes=["rsig"],
                             force_sync=True, tag=f"rex{h}.{p}")
                    yield unit_rcp

                    def unit_nrm(h=h, p=p):
                        pvname = f"ps{4 + p}"

                        def bcmm(t, p=p):
                            return t.matmul(ps[2][0:64, 0:512], ones1[0:1, 0:64],
                                            rsig[0:1, 0:512], start=True, stop=True)
                        P.op("pe", bcmm, reads=["ones1", "rsig"], writes=["ps2"],
                             tag=f"bcmm{h}.{p}")

                        def bcev(v):
                            return v.tensor_copy(bcs[:], ps[2][0:64, 0:512])
                        P.op("dve", bcev, reads=["ps2"], writes=["bcs"],
                             tag=f"bcev{h}.{p}")

                        def nrm(v, h=h, p=p):
                            return v.tensor_tensor(
                                aall[(h % 2) * 64:(h % 2) * 64 + 64,
                                     (h // 2) * T + p * 512:(h // 2) * T + p * 512 + 512],
                                ps[4 + p][0:64, 0:512], bcs[:], ALU.mult)
                        P.op("dve", nrm, reads=[pvname, "bcs"], writes=["aall"],
                             tag=f"nrm{h}.{p}")
                    yield unit_nrm

            def drain(gen):
                for u in gen:
                    u()

            def chain(*gens):
                for g in gens:
                    for u in g:
                        yield u

            def skew2(gen):
                it = iter(gen)
                try:
                    a0 = next(it)
                except StopIteration:
                    return
                pending_b = None
                yield a0
                while True:
                    try:
                        a_or_b = next(it)
                    except StopIteration:
                        break
                    if pending_b is None:
                        pending_b = a_or_b
                        try:
                            nxt_a = next(it)
                        except StopIteration:
                            yield pending_b
                            return
                        yield nxt_a
                        yield pending_b
                        pending_b = None
                    else:
                        raise AssertionError
                return

            def merge(g1, g2, ratio=2, period=1):
                done1 = done2 = False
                it1, it2 = iter(g1), iter(g2)
                step = 0
                while not (done1 and done2):
                    if not done1:
                        try:
                            next(it1)()
                        except StopIteration:
                            done1 = True
                    step += 1
                    if not done2 and step % period == 0:
                        for _ in range(ratio):
                            try:
                                next(it2)()
                            except StopIteration:
                                done2 = True
                                break

            drain(gen_kvq(0))
            drain(gen_stab(0))
            for h in range(NHEADS):
                g2 = skew2(gen_pass2(h)) if h == 0 else chain(gen_norm(h - 1),
                                                              skew2(gen_pass2(h)))
                if h % 2 == 0:
                    merge(g2, gen_stab(h + 1), ratio=1)
                else:
                    beta = h // 2 + 1
                    if beta < NHEADS // 2:
                        nxt = chain(gen_kvq(beta), gen_stab(h + 1))
                        merge(g2, nxt, ratio=1, period=3)
                    else:
                        drain(g2)
            drain(gen_norm(NHEADS - 1))

        P.fence()
        x1bf = E(nc.sbuf_tensor("x1bf", [128, 8 * T], BF16))
        out_acc = E(nc.sbuf_tensor("out_acc", [128, NTILE * D], F32))
        combine = E(nc.sbuf_tensor("combine", [128, 128], F32))
        wob = s1.enter_context(nc.sbuf_tensor("wob", [128, 8 * D], BF16, side="right"))
        gld(wob[:].rearrange("p (hc d) -> p hc d", hc=8),
            wod.rearrange("(hc p) d -> p hc d", p=128), "wob", "ld_wo")
        P.op("gp", lambda g: g.dma_start(out=dummy[0:1, 0:32], in_=ident[0:1, 0:32]),
             writes=["dummy_tail"], tag="gp_pad_wo")
        with ExitStack() as wctx:
            W = wctx.enter_context
            xot = [W(nc.sbuf_tensor(f"xot{i}", [128, D], F32)) for i in range(3)]
            t1 = [W(nc.sbuf_tensor(f"t1_{i}", [128, D], F32)) for i in range(2)]
            x1tmb_ = [W(nc.sbuf_tensor(f"x1tmb{i}", [128, D], BF16)) for i in range(2)]
            psw = [W(nc.psum_tensor(f"psw{i}", [128, 512], F32)) for i in range(4)]
            psb = [W(nc.psum_tensor(f"psb{i}", [128, 512], BF16)) for i in range(2)]

            def ld_xot(tt):
                sld(xot[tt % 3][:],
                    xtm.rearrange("(j p) d -> p j d", p=128)[:, tt, :],
                    f"xot{tt % 3}", f"ld_xo{tt}")

            ld_xot(0)
            ld_xot(1)
            pending_xtr = [None]
            for tt in range(NTILE):
                if tt + 2 < NTILE:
                    ld_xot(tt + 2)
                elif tt == NTILE - 2:
                    P.op("sp", lambda sx: sx.dma_start(
                        out=dummy[0:1, 32:64], in_=ident[0:1, 32:64]),
                        writes=["dummy_sp"], tag="sp_pad_wo")
                xbuf = xot[tt % 3]
                xbname = f"xot{tt % 3}"
                tbuf = t1[tt % 2]
                tbname = f"t1_{tt % 2}"
                for dh in range(2):
                    bank = psw[(2 * tt + dh) % 4]
                    bname = f"psw{(2 * tt + dh) % 4}"

                    def womm(t, bank=bank, tt=tt, dh=dh):
                        for hc in range(8):
                            ins = t.matmul(
                                bank[:],
                                aall[:, hc * T + tt * 128: hc * T + tt * 128 + 128],
                                wob[:, hc * D + dh * 512: hc * D + dh * 512 + 512],
                                start=(hc == 0), stop=(hc == 7))
                        return ins
                    P.op("pe", womm, reads=["aall", "wob"], writes=[bname],
                         tag=f"womm{tt}.{dh}")

                    def wadd(v, bank=bank, tbuf=tbuf, xbuf=xbuf, dh=dh):
                        return v.tensor_tensor(tbuf[:, dh * 512:(dh + 1) * 512],
                                               bank[:], xbuf[:, dh * 512:(dh + 1) * 512],
                                               ALU.add)
                    P.op("dve", wadd, reads=[bname, xbname], writes=[tbname],
                         tag=f"wadd{tt}.{dh}")

                if pending_xtr[0] is not None:
                    pending_xtr[0]()
                    pending_xtr[0] = None

                sc_buf = t1[(tt + 1) % 2]
                sc_name = f"t1_{(tt + 1) % 2}"

                P.op("dve", lambda v, tbuf=tbuf: v.reduce_sum(st_s1, tbuf[:], axis=AX.X),
                     reads=[tbname], writes=["smallA"], force_sync=True, tag=f"ln1s1_{tt}")
                P.op("dve", lambda v: v.tensor_scalar(st_nm, st_s1, -1.0 / D, None,
                                                      ALU.mult),
                     reads=["smallA"], writes=["smallA"], force_sync=True,
                     tag=f"ln1nm_{tt}")
                P.op("dve", lambda v, tbuf=tbuf, sc_buf=sc_buf: v.tensor_scalar(
                    sc_buf[:], tbuf[:], st_nm, None, ALU.add),
                    reads=[tbname, "smallA"], writes=[sc_name], force_sync=True,
                    tag=f"ln1ctr_{tt}")
                P.op("dve", lambda v, sc_buf=sc_buf: v.tensor_tensor(
                    sc_buf[:], sc_buf[:], sc_buf[:], ALU.mult),
                    reads=[sc_name], writes=[sc_name], force_sync=True,
                    tag=f"ln1sq_{tt}")
                P.op("dve", lambda v, sc_buf=sc_buf: v.reduce_sum(
                    st_s2, sc_buf[:], axis=AX.X),
                    reads=[sc_name], writes=["smallA"], force_sync=True,
                    tag=f"ln1s2_{tt}")
                P.op("dve", lambda v: v.tensor_scalar(st_var, st_s2, 1.0 / D, None,
                                                      ALU.mult),
                     reads=["smallA"], writes=["smallA"], force_sync=True,
                     tag=f"ln1var_{tt}")
                P.op("act", lambda sc: sc.activation(st_l, st_var, AF.Ln, bias=st_eps),
                     reads=["smallA"], writes=["smallA"], force_sync=True,
                     tag=f"ln1l_{tt}")
                P.op("act", lambda sc: sc.activation(st_rstd, st_l, AF.Exp, scale=-0.5),
                     reads=["smallA"], writes=["smallA"], force_sync=True,
                     tag=f"ln1r_{tt}")
                P.op("dve", lambda v: v.tensor_scalar(st_rstd2, st_rstd, 2.0, None,
                                                      ALU.mult),
                     reads=["smallA"], writes=["smallA"], force_sync=True,
                     tag=f"ln1r2_{tt}")
                P.op("dve", lambda v, tbuf=tbuf, tt=tt: v.tensor_scalar(
                    out_acc[:, tt * D:(tt + 1) * D], tbuf[:], st_nm, st_rstd2,
                    ALU.add, ALU.mult),
                    reads=[tbname, "smallA"], writes=[f"oacc{tt}"], force_sync=True,
                    tag=f"ln1acc_{tt}")
                x1tmb = x1tmb_[tt % 2]
                x1name = f"x1tmb{tt % 2}"
                P.op("dve", lambda v, tbuf=tbuf, x1tmb=x1tmb: v.tensor_scalar(
                    x1tmb[:], tbuf[:], st_nm, st_rstd, ALU.add, ALU.mult),
                    reads=[tbname, "smallA"], writes=[x1name], force_sync=True,
                    tag=f"ln1x1_{tt}")
                def emit_xtr(tt=tt, x1tmb=x1tmb, x1name=x1name):
                    for half in range(2):
                        bank = psb[half]
                        bname = f"psb{half}"

                        def xtr(t, bank=bank, half=half, x1tmb=x1tmb):
                            for q in range(4):
                                ins = t.transpose(
                                    bank[:, q * 128:(q + 1) * 128],
                                    x1tmb[:, half * 512 + q * 128: half * 512 + (q + 1) * 128],
                                    idnb[:])
                            return ins
                        P.op("pe", xtr, reads=[x1name, "idnb"], writes=[bname],
                             tag=f"xtr{tt}.{half}")

                        def xev(sc, bank=bank, tt=tt, half=half):
                            return sc.activation(
                                x1bf[:].rearrange("p (dc t) -> p dc t", dc=8)
                                [:, half * 4:(half + 1) * 4, tt * 128:(tt + 1) * 128],
                                bank[:].rearrange("p (q c) -> p q c", q=4),
                                AF.Copy)
                        P.op("act", xev, reads=[bname], writes=["x1bf"],
                             tag=f"xev{tt}.{half}")
                pending_xtr[0] = emit_xtr
            pending_xtr[0]()
            pending_xtr[0] = None

        s1.close()
        P.fence()
        wfcb = [E(nc.sbuf_tensor(f"wfcb{i}", [128, 8 * 1024], BF16))
                for i in range(2)]
        wprb = [E(nc.sbuf_tensor(f"wprb{i}", [128, 8 * 1024], BF16))
                for i in range(2)]
        hb = E(nc.sbuf_tensor("hb", [128, 8 * T], BF16))
        psf = [E(nc.psum_tensor(f"psf{i}", [128, 512], F32)) for i in range(2)]
        psp = [E(nc.psum_tensor(f"psp{i}", [128, 512], F32)) for i in range(3)]

        NUNIT = 4
        units = [(e, u) for e in range(NE_R + NE_S) for u in range(NUNIT)]

        def load_unit(idx):
            e, u = units[idx]
            slot = idx % 2
            gld(wfcb[slot][:].rearrange("p (dc f) -> p dc f", dc=8),
                wfc[e].rearrange("(dc p) f -> p dc f", p=128)
                [:, :, u * 1024:(u + 1) * 1024],
                f"wfcb{slot}", f"ldfc{idx}")
            gld(wprb[slot][:].rearrange("p (fc d) -> p fc d", fc=8),
                wpr[e].rearrange("(fc p) d -> p fc d", p=128)
                [:, u * 8:(u + 1) * 8, :],
                f"wprb{slot}", f"ldpr{idx}")

        load_unit(0)
        load_unit(1)
        with ExitStack() as rctx:
            Rr = rctx.enter_context
            centb = Rr(nc.sbuf_tensor("centb", [128, 8 * 8], BF16))
            rbs = Rr(nc.sbuf_tensor("rbs", [128, 128], F32))
            rtmp = Rr(nc.sbuf_tensor("rtmp", [128, 6 * 128 + 64], F32))
            psr = Rr(nc.psum_tensor("psr", [128, 512], F32))

            bal = rtmp[:, 0:128]
            scr = rtmp[:, 128:256]
            rep = rtmp[:, 256:384]
            qtmp = rtmp[:, 384:512]
            msk1 = rtmp[:, 512:640]
            msk2 = rtmp[:, 640:768]
            g1 = rtmp[:, 768:784]
            g2 = rtmp[:, 784:800]
            w1 = rtmp[:, 800:816]
            w2 = rtmp[:, 816:832]

            gld(centb[:].rearrange("p (dc c) -> p dc c", dc=8),
                cent.rearrange("(dc p) c -> p dc c", p=128), "centb", "ld_cent")
            P.op("gp", lambda g: g.dma_start(out=dummy[0:1, 0:32], in_=ident[0:1, 0:32]),
                 writes=["dummy_tail"], tag="gp_pad_rt")
            sld(rbs[:], rbias, "rbs", "ld_rbias")
            P.op("sp", lambda sx: sx.dma_start(
                out=dummy[0:1, 32:64], in_=ident[0:1, 32:64]),
                writes=["dummy_sp"], tag="sp_pad_rt")

            for tt in range(NTILE):
                def rmm(t, tt=tt):
                    for dc in range(8):
                        ins = t.matmul(
                            psr[:, tt * 8:(tt + 1) * 8],
                            x1bf[:, dc * T + tt * 128: dc * T + tt * 128 + 128],
                            centb[:, dc * 8:(dc + 1) * 8],
                            start=(tt == 0 and dc == 0),
                            stop=(tt == NTILE - 1 and dc == 7))
                    return ins
                P.op("pe", rmm, reads=["x1bf", "centb"], writes=["psr"],
                     tag=f"rmm{tt}")
            P.op("act", lambda sc: sc.activation(scr, psr[:, 0:128], AF.Sigmoid),
                 reads=["psr"], writes=["rtmp"], tag="sig_scr")
            P.op("dve", lambda v: v.tensor_tensor(qtmp, psr[:, 0:128], rbs[:],
                                                  ALU.add),
                 reads=["psr", "rbs"], writes=["rtmp"], tag="rawb")
            P.op("act", lambda sc: sc.activation(bal, qtmp, AF.Sigmoid),
                 reads=["rtmp"], writes=["rtmp"], force_sync=True, tag="sig_bal")

            def seg(ap):
                return ap.rearrange("p (s e) -> p s e", e=8)

            def col3(ap):
                return ap.rearrange("p (s one) -> p s one", one=1)

            def dop(fn, tag):
                P.op("dve", fn, reads=["rtmp"], writes=["rtmp"], force_sync=True,
                     tag=tag)

            dop(lambda v: v.reduce_max(col3(g1), seg(bal), axis=AX.X), "r_m1")
            for e in range(8):
                dop(lambda v, e=e: v.tensor_copy(
                    seg(rep)[:, :, e:e + 1], col3(g1)), f"r_rep1{e}")
            dop(lambda v: v.tensor_tensor(msk1, bal, rep, ALU.is_ge), "r_msk1")
            dop(lambda v: v.scalar_tensor_tensor(qtmp, msk1, -1e30, bal,
                                                 ALU.mult, ALU.add), "r_balx")
            dop(lambda v: v.reduce_max(col3(g2), seg(qtmp), axis=AX.X), "r_m2")
            for e in range(8):
                dop(lambda v, e=e: v.tensor_copy(
                    seg(rep)[:, :, e:e + 1], col3(g2)), f"r_rep2{e}")
            dop(lambda v: v.tensor_tensor(msk2, qtmp, rep, ALU.is_ge), "r_msk2")
            dop(lambda v: v.tensor_tensor(qtmp, msk1, scr, ALU.mult), "r_q1")
            dop(lambda v: v.reduce_sum(col3(g1), seg(qtmp), axis=AX.X), "r_g1")
            dop(lambda v: v.tensor_tensor(qtmp, msk2, scr, ALU.mult), "r_q2")
            dop(lambda v: v.reduce_sum(col3(g2), seg(qtmp), axis=AX.X), "r_g2")
            dop(lambda v: v.tensor_tensor(w1, g1, g2, ALU.subtract), "r_d1")
            P.op("act", lambda sc: sc.activation(w2, w1, AF.Sigmoid, scale=-1.0),
                 reads=["rtmp"], writes=["rtmp"], force_sync=True, tag="r_w2")
            P.op("act", lambda sc: sc.activation(w1, w1, AF.Sigmoid),
                 reads=["rtmp"], writes=["rtmp"], force_sync=True, tag="r_w1")
            for e in range(8):
                dop(lambda v, e=e: v.tensor_copy(
                    seg(rep)[:, :, e:e + 1], col3(w1)), f"r_repw1{e}")
            dop(lambda v: v.tensor_tensor(qtmp, msk1, rep, ALU.mult), "r_c1")
            for e in range(8):
                dop(lambda v, e=e: v.tensor_copy(
                    seg(rep)[:, :, e:e + 1], col3(w2)), f"r_repw2{e}")
            dop(lambda v: v.tensor_tensor(rep, msk2, rep, ALU.mult), "r_c2")
            P.op("dve", lambda v: v.tensor_tensor(combine[:], qtmp, rep, ALU.add),
                 reads=["rtmp"], writes=["combine"], force_sync=True, tag="r_comb")

        ysb = [E(nc.sbuf_tensor(f"ysb{i}", [128, D], F32)) for i in range(2)]
        sqb = E(nc.sbuf_tensor("sqb", [128, D], F32))

        def emit_ln2(tt):
            src = out_acc[:, tt * D:(tt + 1) * D]
            oan = f"oacc{tt}"
            P.op("dve", lambda v, src=src: v.reduce_sum(st_s1, src, axis=AX.X),
                 reads=[oan], writes=["smallA"], force_sync=True,
                 tag=f"ln2s1_{tt}")
            P.op("dve", lambda v: v.tensor_scalar(st_nm, st_s1, -1.0 / D, None,
                                                  ALU.mult),
                 reads=["smallA"], writes=["smallA"], force_sync=True,
                 tag=f"ln2nm_{tt}")
            P.op("dve", lambda v, src=src: v.tensor_scalar(
                sqb[:], src, st_nm, None, ALU.add),
                reads=[oan, "smallA"], writes=["sqb"], force_sync=True,
                tag=f"ln2ctr_{tt}")
            P.op("dve", lambda v: v.tensor_tensor(sqb[:], sqb[:], sqb[:],
                                                  ALU.mult),
                 reads=["sqb"], writes=["sqb"], force_sync=True,
                 tag=f"ln2sq_{tt}")
            P.op("dve", lambda v: v.reduce_sum(st_s2, sqb[:], axis=AX.X),
                 reads=["sqb"], writes=["smallA"], force_sync=True,
                 tag=f"ln2s2_{tt}")
            P.op("dve", lambda v: v.tensor_scalar(st_var, st_s2, 1.0 / D, None,
                                                  ALU.mult),
                 reads=["smallA"], writes=["smallA"], force_sync=True,
                 tag=f"ln2var_{tt}")
            P.op("act", lambda sc: sc.activation(st_l, st_var, AF.Ln, bias=st_eps),
                 reads=["smallA"], writes=["smallA"], force_sync=True,
                 tag=f"ln2l_{tt}")
            P.op("act", lambda sc: sc.activation(st_rstd, st_l, AF.Exp, scale=-0.5),
                 reads=["smallA"], writes=["smallA"], force_sync=True,
                 tag=f"ln2r_{tt}")
            ybuf = ysb[tt % 2]
            ybname = f"ysb{tt % 2}"
            P.op("dve", lambda v, src=src, ybuf=ybuf: v.tensor_scalar(
                ybuf[:], src, st_nm, st_rstd, ALU.add, ALU.mult),
                reads=[oan, "smallA"], writes=[ybname], force_sync=True,
                tag=f"ln2y_{tt}")
            P.op("dve", lambda v, ybuf=ybuf: v.tensor_copy(ybuf[:, 0:1],
                                                           ybuf[:, 0:1]),
                 reads=[ybname], writes=[ybname], force_sync=True,
                 tag=f"ln2settle_{tt}")
            P.op("sp", lambda s, ybuf=ybuf, tt=tt: s.dma_start(
                out=y.rearrange("(j p) d -> p j d", p=128)[:, tt, :],
                in_=ybuf[:]),
                reads=[ybname], writes=["y"], tag=f"st{tt}")

        with ExitStack() as fctx:
            fcg = [0]
            prg = [0]
            for idx, (e, u) in enumerate(units):
                slot = idx % 2
                wf, wp = wfcb[slot], wprb[slot]
                wfn, wpn = f"wfcb{slot}", f"wprb{slot}"
                for Ft in range(8):
                    for tc in range(4):
                        bank = psf[fcg[0] % 2]
                        bname = f"psf{fcg[0] % 2}"
                        fcg[0] += 1

                        def fcmm(t, bank=bank, wf=wf, Ft=Ft, tc=tc):
                            for dc in range(8):
                                ins = t.matmul(
                                    bank[:],
                                    wf[:, dc * 1024 + Ft * 128: dc * 1024 + (Ft + 1) * 128],
                                    x1bf[:, dc * T + tc * 512: dc * T + tc * 512 + 512],
                                    start=(dc == 0), stop=(dc == 7))
                            return ins
                        P.op("pe", fcmm, reads=[wfn, "x1bf"], writes=[bname],
                             tag=f"fc{idx}.{Ft}.{tc}")

                        def gel(sc, bank=bank, Ft=Ft, tc=tc):
                            return sc.activation(
                                hb[:, Ft * T + tc * 512: Ft * T + tc * 512 + 512],
                                bank[:], AF.Gelu)
                        P.op("act", gel, reads=[bname], writes=["hb"],
                             tag=f"gel{idx}.{Ft}.{tc}")
                for tt in range(NTILE):
                    for dh in range(2):
                        bank = psp[prg[0] % 3]
                        bname = f"psp{prg[0] % 3}"
                        prg[0] += 1

                        def prmm(t, bank=bank, wp=wp, tt=tt, dh=dh):
                            for fc_ in range(8):
                                ins = t.matmul(
                                    bank[:],
                                    hb[:, fc_ * T + tt * 128: fc_ * T + tt * 128 + 128],
                                    wp[:, fc_ * 1024 + dh * 512: fc_ * 1024 + dh * 512 + 512],
                                    start=(fc_ == 0), stop=(fc_ == 7))
                            return ins
                        P.op("pe", prmm, reads=["hb", wpn], writes=[bname],
                             tag=f"pr{idx}.{tt}.{dh}")

                        if e < NE_R:
                            def ev(v, bank=bank, tt=tt, dh=dh, e=e):
                                dst = out_acc[:, tt * D + dh * 512:
                                              tt * D + dh * 512 + 512]
                                return v.scalar_tensor_tensor(
                                    dst, bank[:], combine[:, tt * 8 + e:tt * 8 + e + 1],
                                    dst, ALU.mult, ALU.add)
                            P.op("dve", ev, reads=[bname, "combine", f"oacc{tt}"],
                                 writes=[f"oacc{tt}"], tag=f"ev{idx}.{tt}.{dh}")
                        else:
                            def ev(v, bank=bank, tt=tt, dh=dh):
                                dst = out_acc[:, tt * D + dh * 512:
                                              tt * D + dh * 512 + 512]
                                return v.tensor_tensor(dst, bank[:], dst, ALU.add)
                            P.op("dve", ev, reads=[bname, f"oacc{tt}"],
                                 writes=[f"oacc{tt}"], tag=f"ev{idx}.{tt}.{dh}")
                        if idx == len(units) - 1 and dh == 1:
                            emit_ln2(tt)
                if idx + 2 < len(units):
                    load_unit(idx + 2)
                    if idx + 2 == len(units) - 1:
                        P.op("gp", lambda g: g.dma_start(
                            out=dummy[0:1, 0:32], in_=ident[0:1, 0:32]),
                            writes=["dummy_tail"], tag="gp_pad_ffn")

        sp_final = P.counts["sp"]

        @block.tensor
        def _(t):
            P.emit_engine("pe", t)

        @block.scalar
        def _(sc):
            P.emit_engine("act", sc)

        @block.vector
        def _(v):
            P.emit_engine("dve", v)

        @block.sync
        def _(s):
            P.emit_engine("sp", s)
            s.wait_ge(P.sems["sp"], sp_final)

        @block.gpsimd
        def _(g):
            P.emit_engine("gp", g)

    return nc


_NC_CACHE = None


def kernel(L0_attn_x, L0_attn_WQ, L0_attn_WK, L0_attn_WV, L0_attn_WO,
           L0_ffn_centroids, L0_ffn_routing_bias, L0_ffn_Wfc_r, L0_ffn_Wproj_r,
           L0_ffn_Wfc_sh, L0_ffn_Wproj_sh):
    global _NC_CACHE
    x = np.asarray(L0_attn_x, np.float32)

    wq = np.ascontiguousarray(np.asarray(L0_attn_WQ, np.float32)
                              .transpose(2, 0, 1).reshape(D, D))
    wk = np.ascontiguousarray(np.asarray(L0_attn_WK, np.float32)
                              .transpose(2, 0, 1).reshape(D, D))
    wv = np.ascontiguousarray(np.asarray(L0_attn_WV, np.float32)
                              .transpose(2, 0, 1).reshape(D, D))
    wo = np.ascontiguousarray(np.asarray(L0_attn_WO, np.float32)
                              .transpose(0, 2, 1).reshape(D, D))
    cent = np.ascontiguousarray(np.asarray(L0_ffn_centroids, np.float32).T)
    rbias = np.ascontiguousarray(np.broadcast_to(
        np.tile(np.asarray(L0_ffn_routing_bias, np.float32), NTILE)[None, :],
        (128, 128)))
    wfc = np.ascontiguousarray(np.concatenate(
        [np.asarray(L0_ffn_Wfc_r, np.float32),
         np.asarray(L0_ffn_Wfc_sh, np.float32)], axis=0).transpose(0, 2, 1))
    wpr = np.ascontiguousarray(np.concatenate(
        [np.asarray(L0_ffn_Wproj_r, np.float32),
         np.asarray(L0_ffn_Wproj_sh, np.float32)], axis=0).transpose(0, 2, 1))
    ident = np.eye(128, dtype=np.float32)
    identb = np.eye(128, dtype=ml_dtypes.bfloat16)

    xts = [np.ascontiguousarray(x[b].T) for b in range(2)]

    in_maps = []
    for c in range(8):
        b, k = c // 4, c % 4
        rows = np.concatenate(
            [np.arange(128) + 512 * j + 128 * k for j in range(NTILE)])
        xtm_c = np.ascontiguousarray(x[b][rows, :])
        xt_own = np.ascontiguousarray(xts[b][:, rows])
        mask = np.zeros((4, 128, 128), np.float32)
        for bb in range(4):
            kk = np.arange(128)[:, None]
            rr = np.arange(128)[None, :]
            mask[bb] = np.where(128 * bb + kk <= 128 * k + rr, 0.0, -1e30)
        in_maps.append(dict(
            xt=xts[b], xtm=xtm_c, xt_own=xt_own,
            wq=wq, wk=wk, wv=wv, wo=wo, cent=cent, rbias=rbias,
            wfc=wfc, wpr=wpr,
            mask01=mask.astype(ml_dtypes.bfloat16), ident=ident, identb=identb,
        ))

    if _NC_CACHE is None:
        _NC_CACHE = build_program()
    res = run_bass_kernel_spmd(_NC_CACHE, in_maps, core_ids=list(range(8)))

    out = np.empty((2, S, D), np.float32)
    for c in range(8):
        b, k = c // 4, c % 4
        yc = np.asarray(res.results[c]["y"], np.float32)
        for j in range(NTILE):
            out[b, 512 * j + 128 * k: 512 * j + 128 * k + 128, :] = \
                yc[j * 128:(j + 1) * 128, :]
    return out
